# version 35
# baseline (speedup 1.0000x reference)
from contextlib import ExitStack
import numpy as np
import concourse.bass as bass
import concourse.mybir as mybir

F32 = mybir.dt.float32
BF16 = mybir.dt.bfloat16
AF = mybir.ActivationFunctionType
ALU = mybir.AluOpType
AX = mybir.AxisListType

D = 128
LN_EPS = 1e-5
RMS_EPS = float(np.finfo(np.float32).eps)
VN_EPS2 = 1e-24


def halves(total, w=512):
    return [(o, min(w, total - o)) for o in range(0, total, w)]


def build_core(C, NN, QL, q0, FF, H, KVH, SEG=512, GHmax=8, NR=1, NCORES=1):
    CC = C // D
    SEGN = NN // SEG
    SEGC = SEG // D
    KCH = NN // D
    QT = QL // D
    FFC = FF // D
    KVD = KVH * D
    REP = H // KVH
    QH = halves(QL)
    OSW = min(256, H * D)
    NOS = (H * D) // OSW
    HPS = OSW // D
    HSW = min(128, FF)
    NHS = FF // HSW
    HPG = HSW // D
    GH = min(GHmax, FFC)
    NG = FFC // GH
    CH = halves(C)
    COLW = CH[0][1]
    NCQ = len(CH)
    own_s0 = 0
    own_sn = QL // SEG
    PSEGN = QL // SEG
    assert q0 == 0 and NN == NR * QL
    SCALE = float(1.0 / np.sqrt(D))
    assert KVH % 2 == 0 and SEGC % 2 == 0 and QT <= 8

    nc = bass.Bass("TRN2", target_bir_lowering=False)
    xT = nc.dram_tensor("xT", [C, QL], F32, kind="ExternalInput").ap()
    xr = nc.dram_tensor("xr", [QL, C], F32, kind="ExternalInput").ap()
    wq = nc.dram_tensor("wq", [C, H * D], F32, kind="ExternalInput").ap()
    wk = nc.dram_tensor("wk", [C, KVD], F32, kind="ExternalInput").ap()
    wv = nc.dram_tensor("wv", [C, KVD], F32, kind="ExternalInput").ap()
    wo = nc.dram_tensor("wo", [H * D, C], F32, kind="ExternalInput").ap()
    wg = nc.dram_tensor("wg", [C, FF], F32, kind="ExternalInput").ap()
    wu = nc.dram_tensor("wu", [C, FF], F32, kind="ExternalInput").ap()
    wd = nc.dram_tensor("wd", [FF, C], F32, kind="ExternalInput").ap()
    out = nc.dram_tensor("out", [QL, C], F32, kind="ExternalOutput").ap()
    scr1 = nc.dram_tensor("scr1", [QL, C], F32, kind="Internal").ap()
    st_k = nc.dram_tensor("st_k", [KVH * D, QL], BF16, kind="Internal").ap()
    st_v = nc.dram_tensor("st_v", [(QL // D) * D, KVD], BF16, kind="Internal").ap()
    G_k = nc.dram_tensor("G_k", [NR * KVH * D, QL], BF16, kind="Internal").ap()
    G_v = nc.dram_tensor("G_v", [NR * QL, KVD], BF16, kind="Internal").ap()

    es = ExitStack()
    with es:
        sb = lambda nm, shp, dt: es.enter_context(nc.sbuf_tensor(nm, shp, dt))
        sem = lambda nm: es.enter_context(nc.semaphore(nm))
        xnT = sb("xnT", [128, CC * QL], BF16)
        ones_bf = sb("ones_bf", [128, 1], BF16)
        ones_f = sb("ones_f", [128, 1], F32)
        onesr_f = sb("onesr_f", [1, 128], F32)
        onesr_bf = sb("onesr_bf", [1, 128], BF16)
        ident = sb("ident", [128, 128], BF16)
        eps_ln = sb("eps_ln", [1, 1], F32)
        eps_rms = sb("eps_rms", [1, 1], F32)

        names = ("gI dWKV dXA0 dXA1 aX2 pSTh dGK dGV ccK ccV dRK dRV dVAR aRSTD dMR pBC aBCE dAPP pAPP "
                 "pK aK2 pKMS aKR pKBC aKBE dK pV aV "
                 "aV2 dSS aVR aVS pVT dVN "
                 "dWQH dWQ0 dWQ1 fWQ pQ aQ2 pQMS aQR pQBC aQBE dQ "
                 "pS aE pPV dA dP gQ dT dYE pROW dROWd aR1 pBC2 aR2 dY "
                 "dWO0 dWO1 pW dXR0 dXR1 dXR2 dXR3 dW dSC0 dSC1 "
                 "dWG0 dWG1 fWG pGU aSIL dGT dWD0 dWD1 fWD pDN dSC20 dSC21 dSC22 dSC23 dF dOUT0 dOUT1 "
                 "bar1 bar1g bar2 bar3 bar4 "
                 "slfT slfS slfV slfG slfY").split()
        S = {n: sem(n) for n in names}
        slf_cnt = {"tensor": 0, "scalar": 0, "vector": 0, "gpsimd": 0, "sync": 0}
        slf_sem = {"tensor": S["slfT"], "scalar": S["slfS"], "vector": S["slfV"],
                   "gpsimd": S["slfG"], "sync": S["slfY"]}

        def fence(en, e, ins):
            slf_cnt[en] += 1
            ins.then_inc(slf_sem[en], 1)
            e.wait_ge(slf_sem[en], slf_cnt[en])

        import contextlib
        _outer_blk = es.enter_context(nc.Block())

        def Block_():
            return contextlib.nullcontext(_outer_blk)

        def cnt_pK(s, kv):
            return s * KVH + kv + 1

        def cnt_pV(s, j):
            return s * SEGC + j + 1

        def is_own(s):
            return own_s0 <= s < own_s0 + own_sn

        def own_idx(s, j):
            return (s - own_s0) * SEGC + j


        with Block_() as blk:
            @blk.gpsimd
            def _(g):
                g.memset(ones_bf[:, :], 1.0).then_inc(S["gI"], 1)
                g.memset(ones_f[:, :], 1.0).then_inc(S["gI"], 1)
                g.memset(onesr_f[:1, :], 1.0).then_inc(S["gI"], 1)
                g.memset(onesr_bf[:1, :], 1.0).then_inc(S["gI"], 1)
                g.memset(ident[:, :], 0.0).then_inc(S["gI"], 1)
                g.memset(eps_ln[:1, :], LN_EPS).then_inc(S["gI"], 1)
                g.memset(eps_rms[:1, :], RMS_EPS).then_inc(S["gI"], 1)
                g.wait_ge(S["gI"], 7)
                g.affine_select(out=ident[:, :], in_=ident[:, :],
                                compare_op=ALU.not_equal, fill=1.0, base=0,
                                pattern=[[-1, 128]], channel_multiplier=1
                                ).then_inc(S["gI"], 1)

        es2 = ExitStack()
        with es2:
            sb2 = lambda nm, shp, dt: es2.enter_context(nc.sbuf_tensor(nm, shp, dt))
            kT = sb2("kT", [128, KVH * NN], BF16)
            vT = sb2("vT", [128, KCH * KVD], BF16)
            vnT = sb2("vnT", [128, KVH * QL], BF16)
            wqh = sb2("wqh", [128, CC * D], BF16)

            esP = ExitStack()
            with esP:
                sbP = lambda nm, shp, dt: esP.enter_context(nc.sbuf_tensor(nm, shp, dt))
                psP = lambda nm, shp, dt=F32: esP.enter_context(nc.psum_tensor(nm, shp, dt))
                xA = [sbP("xA0", [128, CC * SEG], BF16)]
                x2h = sbP("x2h", [128, (CC // 2) * SEG], BF16)
                kT_own = sbP("kT_own", [128, KVH * QL], BF16)
                vT_own = sbP("vT_own", [128, (QL // D) * KVD], BF16)
                wk_bf = sbP("wk_bf", [128, CC * KVD], BF16)
                wv_bf = sbP("wv_bf", [128, CC * KVD], BF16)
                mean_r = sbP("mean_r", [1, SEG], F32)
                ms_r = sbP("ms_r", [1, SEG], F32)
                var_r = sbP("var_r", [1, SEG], F32)
                lnv_r = sbP("lnv_r", [1, SEG], F32)
                rstd_r = sbP("rstd_r", [1, SEG], F32)
                mrstd_r = sbP("mrstd_r", [1, SEG], F32)
                rstd_bc = sbP("rstd_bc", [128, SEG], BF16)
                mrstd_bc = sbP("mrstd_bc", [128, SEG], BF16)
                tmp_bf = sbP("tmp_bf", [128, SEG], BF16)
                k2_bf = sbP("k2_bf", [128, SEG], BF16)
                klnr = sbP("klnr", [1, SEG], F32)
                krstd_r = sbP("krstd_r", [1, SEG], F32)
                krstd_bc = sbP("krstd_bc", [128, SEG], BF16)
                v2_bf = sbP("v2_bf", [128, KVD], BF16)
                ss_c = sbP("ss_c", [128, KVH], F32)
                ssm_c = sbP("ssm_c", [128, KVH], F32)
                sln_c = sbP("sln_c", [128, KVH], F32)
                vrstd_c = sbP("vrstd_c", [128, KVH], F32)
                vn_buf = sbP("vn_buf", [128, KVH * D], BF16)
                pA = psP("pA", [1, SEG])
                pB = psP("pB", [1, SEG])
                pC = psP("pC", [128, SEG])
                pE = psP("pE", [128, SEG])
                pF = psP("pF", [128, SEG])
                pD = psP("pD", [128, SEG])
                pG = psP("pG", [1, SEG])
                pTR = psP("pTR", [128, D], BF16)

                def xnT_src(s):
                    return xnT, (s - own_s0) * SEG, QL

                def xnt_ap(s, cc):
                    xnt, xb, xstr = xnT_src(s)
                    return xnt[:, xb + cc * xstr: xb + cc * xstr + SEG]

                def xnt_stat_ap(s, cc, j):
                    xnt, xb, xstr = xnT_src(s)
                    return xnt[:, xb + cc * xstr + j * D: xb + cc * xstr + (j + 1) * D]

                def xsrc(s, cc):
                    if s == 0:
                        return xA[0][:, cc * SEG:(cc + 1) * SEG]
                    return xnt_ap(s, cc)

                def xsrc_half(s, half):
                    HCC = CC // 2
                    if s == 0:
                        return xA[0][:, half * HCC * SEG:(half + 1) * HCC * SEG]
                    xnt, xb, xstr = xnT_src(s)
                    return xnt.rearrange("p (cc q) -> p cc q", q=xstr)[
                        :, half * HCC:(half + 1) * HCC, xb:xb + SEG]

                def vslot(s, j):
                    return pE if (KVH + j) % 2 == 0 else pF

                def kslot(kv):
                    return pE if kv % 2 == 0 else pF

                with Block_() as blk:
                    POOL_CC = CC // 4
                    DVE_CC = CC - POOL_CC

                    @blk.gpsimd
                    def _(g):
                        en = "gpsimd"
                        g.dma_start(
                            xA[0][:, :].rearrange("p (cc n) -> p cc n", n=SEG),
                            xT[:, 0:SEG].rearrange("(cc p) n -> p cc n", p=128),
                        ).then_inc(S["dXA0"], 16)
                        g.dma_start(wk_bf[:, :].rearrange("p (cc o) -> p cc o", o=KVD),
                                    wk.rearrange("(cc p) o -> p cc o", p=128)
                                    ).then_inc(S["dWKV"], 16)
                        g.dma_start(wv_bf[:, :].rearrange("p (cc o) -> p cc o", o=KVD),
                                    wv.rearrange("(cc p) o -> p cc o", p=128)
                                    ).then_inc(S["dWKV"], 16)
                        for s2 in range(1, PSEGN):
                            xnt2, xb2, xstr2 = xnT_src(s2)
                            g.dma_start(
                                xnt2.rearrange("p (cc q) -> p cc q", q=xstr2)[
                                    :, :, xb2:xb2 + SEG],
                                xT[:, s2 * SEG:(s2 + 1) * SEG].rearrange(
                                    "(cc p) n -> p cc n", p=128),
                            ).then_inc(S["dXA1"], 16)
                        for s in range(PSEGN):
                            xnt, xb, xstr = xnT_src(s)
                            if POOL_CC:
                                g.wait_ge(S["pSTh"], 2 * s + 2)
                                g.wait_ge(S["aBCE"], 2 * s + 2)
                                for ci in range(POOL_CC):
                                    cc = DVE_CC + ci
                                    ins = g.tensor_tensor(
                                        x2h[:, 0:SEG], xsrc(s, cc),
                                        rstd_bc[:, :], ALU.mult)
                                    fence(en, g, ins)
                                    ins = g.tensor_tensor(
                                        xnt[:, xb + cc * xstr: xb + cc * xstr + SEG],
                                        x2h[:, 0:SEG], mrstd_bc[:, :], ALU.subtract)
                                    if ci == POOL_CC - 1:
                                        ins.then_inc(S["pAPP"], 1)
                                    else:
                                        fence(en, g, ins)
                            else:
                                g.sem_inc(S["pAPP"], 1)
                        g.sem_inc(S["bar1"], 1)
                        g.dma_start(
                            wqh[:, :].rearrange("p (cc o) -> p cc o", o=D),
                            wq[:, 0:D].rearrange("(cc p) o -> p cc o", p=128),
                        ).then_inc(S["dWQH"], 16)
                        g.wait_ge(S["dK"], PSEGN * KVH)
                        g.wait_ge(S["aV"], PSEGN * SEGC)
                        g.dma_start(st_k.rearrange("(kv d) t -> d kv t", d=128),
                                    kT_own[:, :].rearrange("d (kv t) -> d kv t", t=QL)
                                    ).then_inc(S["dGK"], 16)
                        g.dma_start(st_v.rearrange("(j p) o -> p j o", p=128),
                                    vT_own[:, :].rearrange("p (j o) -> p j o", o=KVD)
                                    ).then_inc(S["dGV"], 16)
                        g.wait_ge(S["dGK"], 16)
                        g.wait_ge(S["dGV"], 16)
                        if NR > 1:
                            groups = [[gr * NR + i for i in range(NR)]
                                      for gr in range(NCORES // NR)]
                            g.collective_compute(
                                "AllGather", ALU.bypass, replica_groups=groups,
                                ins=[st_k.opt()], outs=[G_k.opt()],
                            ).then_inc(S["ccK"], 1)
                            g.collective_compute(
                                "AllGather", ALU.bypass, replica_groups=groups,
                                ins=[st_v.opt()], outs=[G_v.opt()],
                            ).then_inc(S["ccV"], 1)
                            gk_src, gv_src = G_k, G_v
                        else:
                            gk_src, gv_src = st_k, st_v

                    @blk.sync
                    def _(sy):
                        if NR > 1:
                            sy.wait_ge(S["ccK"], 1)
                            sy.wait_ge(S["ccV"], 1)
                        else:
                            sy.wait_ge(S["dGK"], 16)
                            sy.wait_ge(S["dGV"], 16)
                        gks = G_k if NR > 1 else st_k
                        gvs = G_v if NR > 1 else st_v
                        QT_ = QL // D
                        for r in range(NR):
                            sy.dma_start(
                                kT[:, :].rearrange("d (kv n) -> d kv n", n=NN)[
                                    :, :, r * QL:(r + 1) * QL],
                                gks[r * KVH * D:(r + 1) * KVH * D, :].rearrange(
                                    "(kv d) t -> d kv t", d=128)
                            ).then_inc(S["dRK"], 16)
                            sy.dma_start(
                                vT[:, (r * QT_) * KVD:((r + 1) * QT_) * KVD].rearrange(
                                    "p (j o) -> p j o", o=KVD),
                                gvs[r * QL:(r + 1) * QL, :].rearrange(
                                    "(j p) o -> p j o", p=128)
                            ).then_inc(S["dRV"], 16)


                    @blk.tensor
                    def _(t):
                        t.wait_ge(S["gI"], 8)
                        last = [None]
                        for s in range(PSEGN):
                            if s == 0:
                                t.wait_ge(S["dXA0"], 16)
                            else:
                                t.wait_ge(S["dXA1"], 16 * s)
                                t.wait_ge(S["dVAR"], s)
                            for cc in range(CC):
                                t.matmul(pA[:1, :], ones_bf[:, :],
                                         xsrc(s, cc),
                                         start=(cc == 0), stop=(cc == CC - 1))
                            HCC = CC // 2
                            for half in range(2):
                                t.wait_ge(S["aX2"], 2 * s + half + 1)
                                for c2 in range(HCC):
                                    cc = half * HCC + c2
                                    ins = t.matmul(pB[:1, :], ones_bf[:, :],
                                                   x2h[:, c2 * SEG:(c2 + 1) * SEG],
                                                   start=(cc == 0), stop=(cc == CC - 1))
                                ins.then_inc(S["pSTh"], 1)
                            t.wait_ge(S["aRSTD"], s + 1)
                            if s >= 1:
                                t.wait_ge(S["aBCE"], 2 * s)
                            t.matmul(pC[:, :], onesr_f[:1, :], rstd_r[:1, :],
                                     start=True, stop=True).then_inc(S["pBC"], 1)
                            t.wait_ge(S["dMR"], s + 1)
                            t.wait_ge(S["aBCE"], 2 * s + 1)
                            t.matmul(pC[:, :], onesr_f[:1, :], mrstd_r[:1, :],
                                     start=True, stop=True).then_inc(S["pBC"], 1)
                            if s == 0:
                                t.wait_ge(S["dWKV"], 32)
                            for kv in range(KVH):
                                idx = cnt_pK(s, kv)
                                t.wait_ge(S["dAPP"], s + 1)
                                t.wait_ge(S["pAPP"], s + 1)
                                slot = kslot(kv)
                                if kv >= 2:
                                    t.wait_ge(S["dK"], idx - 2)
                                elif s >= 1:
                                    t.wait_ge(S["aV"], cnt_pV(s - 1, SEGC - 2 + kv))
                                for cc in range(CC):
                                    ins = t.matmul(
                                        slot[:, :],
                                        wk_bf[:, cc * KVD + kv * D: cc * KVD + (kv + 1) * D],
                                        xnt_ap(s, cc),
                                        start=(cc == 0), stop=(cc == CC - 1))
                                ins.then_inc(S["pK"], 1)
                                t.wait_ge(S["aK2"], idx)
                                if idx > 1:
                                    t.wait_ge(S["aKR"], idx - 1)
                                t.matmul(pG[:1, :], ones_bf[:, :], k2_bf[:, :],
                                         start=True, stop=True).then_inc(S["pKMS"], 1)
                                t.wait_ge(S["aKR"], idx)
                                if idx > 1:
                                    t.wait_ge(S["aKBE"], idx - 1)
                                t.matmul(pD[:, :], onesr_f[:1, :], krstd_r[:1, :],
                                         start=True, stop=True).then_inc(S["pKBC"], 1)
                            for j in range(SEGC):
                                jdx = cnt_pV(s, j)
                                slot = vslot(s, j)
                                if j < 2:
                                    t.wait_ge(S["dK"], s * KVH + (KVH - 2 + j) + 1)
                                else:
                                    t.wait_ge(S["aV"], jdx - 2)
                                for cc in range(CC):
                                    ins = t.matmul(
                                        slot[:, :KVD],
                                        xnt_stat_ap(s, cc, j),
                                        wv_bf[:, cc * KVD: (cc + 1) * KVD],
                                        start=(cc == 0), stop=(cc == CC - 1))
                                ins.then_inc(S["pV"], 1)
                            if is_own(s):
                                for j in range(SEGC):
                                    for kv in range(KVH):
                                        ii = own_idx(s, j) * KVH + kv
                                        t.wait_ge(S["aVS"], ii + 1)
                                        if ii > 0:
                                            t.wait_ge(S["dVN"], ii)
                                        last[0] = t.transpose(
                                            pTR[:, :], vn_buf[:, kv * D:(kv + 1) * D],
                                            ident[:, :])
                                        last[0].then_inc(S["pVT"], 1)
                        t.nop().then_inc(S["bar1"], 1)

                    @blk.scalar
                    def _(a):
                        en = "scalar"
                        last = [None]
                        a.wait_ge(S["gI"], 8)
                        for s in range(PSEGN):
                            if s == 0:
                                a.wait_ge(S["dXA0"], 16)
                            else:
                                a.wait_ge(S["dXA1"], 16 * s)
                                a.wait_ge(S["pAPP"], s)
                            for half in range(2):
                                if 2 * s + half > 0:
                                    a.wait_ge(S["pSTh"], 2 * s + half)
                                a.activation(
                                    x2h[:, :],
                                    xsrc_half(s, half),
                                    AF.Square).then_inc(S["aX2"], 1)
                            a.wait_ge(S["dVAR"], s + 1)
                            ins = a.activation(lnv_r[:1, :], var_r[:1, :], AF.Ln,
                                               bias=eps_ln[:1, :])
                            fence(en, a, ins)
                            if s >= 1:
                                a.wait_ge(S["pBC"], 2 * s)
                                a.wait_ge(S["dMR"], s)
                            a.activation(rstd_r[:1, :], lnv_r[:1, :], AF.Exp,
                                         scale=-0.5).then_inc(S["aRSTD"], 1)
                            a.wait_ge(S["pBC"], 2 * s + 1)
                            if s >= 1:
                                a.wait_ge(S["dAPP"], s)
                            a.activation(rstd_bc[:, :], pC[:, :], AF.Identity
                                         ).then_inc(S["aBCE"], 1)
                            a.wait_ge(S["pBC"], 2 * s + 2)
                            if s >= 1:
                                a.wait_ge(S["dAPP"], s)
                            a.activation(mrstd_bc[:, :], pC[:, :], AF.Identity
                                         ).then_inc(S["aBCE"], 1)
                            for kv in range(KVH):
                                idx = cnt_pK(s, kv)
                                slot = kslot(kv)
                                a.wait_ge(S["pK"], idx)
                                if idx > 1:
                                    a.wait_ge(S["pKMS"], idx - 1)
                                a.activation(k2_bf[:, :], slot[:, :], AF.Square
                                             ).then_inc(S["aK2"], 1)
                                a.wait_ge(S["pKMS"], idx)
                                if idx > 1:
                                    a.wait_ge(S["pKBC"], idx - 1)
                                ins = a.activation(klnr[:1, :], pG[:1, :], AF.Ln,
                                                   scale=1.0 / D, bias=eps_rms[:1, :])
                                fence(en, a, ins)
                                a.activation(krstd_r[:1, :], klnr[:1, :], AF.Exp,
                                             scale=-0.5).then_inc(S["aKR"], 1)
                                a.wait_ge(S["pKBC"], idx)
                                if idx > 1:
                                    a.wait_ge(S["dK"], idx - 1)
                                a.activation(krstd_bc[:, :], pD[:, :], AF.Identity
                                             ).then_inc(S["aKBE"], 1)
                            for j in range(SEGC):
                                jdx = cnt_pV(s, j)
                                slot = vslot(s, j)
                                kpos = s * SEGC + j
                                a.wait_ge(S["pV"], jdx)
                                last[0] = a.activation(
                                    vT_own[:, kpos * KVD:(kpos + 1) * KVD],
                                    slot[:, :KVD], AF.Identity)
                                last[0].then_inc(S["aV"], 1)
                                if is_own(s):
                                    i = own_idx(s, j)
                                    if i > 0:
                                        a.wait_ge(S["dSS"], i)
                                    a.wait_ge(S["aV"], jdx)
                                    a.activation(v2_bf[:, :],
                                                 vT_own[:, kpos * KVD:(kpos + 1) * KVD],
                                                 AF.Square).then_inc(S["aV2"], 1)
                                    a.wait_ge(S["dSS"], i + 1)
                                    if i > 0:
                                        a.wait_ge(S["aVS"], i * KVH)
                                    ins = a.activation(sln_c[:, :], ssm_c[:, :], AF.Ln)
                                    fence(en, a, ins)
                                    a.activation(vrstd_c[:, :], sln_c[:, :], AF.Exp,
                                                 scale=-0.5).then_inc(S["aVR"], 1)
                                    a.wait_ge(S["aVR"], i + 1)
                                    if i > 0:
                                        a.wait_ge(S["pVT"], i * KVH)
                                    for kv in range(KVH):
                                        last[0] = a.activation(
                                            vn_buf[:, kv * D:(kv + 1) * D],
                                            vT_own[:, kpos * KVD + kv * D: kpos * KVD + (kv + 1) * D],
                                            AF.Identity, scale=vrstd_c[:, kv:kv + 1])
                                        last[0].then_inc(S["aVS"], 1)
                        a.drain().then_inc(S["bar1"], 1)

                    @blk.vector
                    def _(v):
                        en = "vector"
                        last = [None]
                        for s in range(PSEGN):
                            v.wait_ge(S["pSTh"], 2 * s + 2)
                            if s >= 1:
                                v.wait_ge(S["pBC"], 2 * s)
                                v.wait_ge(S["aRSTD"], s)
                            ins = v.tensor_scalar(mean_r[:1, :], pA[:1, :], 1.0 / C,
                                                  None, ALU.mult)
                            fence(en, v, ins)
                            ins = v.tensor_scalar(ms_r[:1, :], pB[:1, :], 1.0 / C,
                                                  None, ALU.mult)
                            fence(en, v, ins)
                            ins = v.tensor_tensor(var_r[:1, :], mean_r[:1, :],
                                                  mean_r[:1, :], ALU.mult)
                            fence(en, v, ins)
                            ins = v.tensor_tensor(var_r[:1, :], ms_r[:1, :],
                                                  var_r[:1, :], ALU.subtract)
                            ins.then_inc(S["dVAR"], 1)
                            v.wait_ge(S["aRSTD"], s + 1)
                            v.tensor_tensor(mrstd_r[:1, :], mean_r[:1, :],
                                            rstd_r[:1, :], ALU.mult).then_inc(S["dMR"], 1)
                            xnt, xb, xstr = xnT_src(s)
                            v.wait_ge(S["aBCE"], 2 * s + 2)
                            for cc in range(DVE_CC):
                                ins = v.tensor_tensor(
                                    tmp_bf[:, :], xsrc(s, cc),
                                    rstd_bc[:, :], ALU.mult)
                                fence(en, v, ins)
                                ins = v.tensor_tensor(
                                    xnt[:, xb + cc * xstr: xb + cc * xstr + SEG],
                                    tmp_bf[:, :], mrstd_bc[:, :], ALU.subtract)
                                if cc < DVE_CC - 1:
                                    fence(en, v, ins)
                            ins.then_inc(S["dAPP"], 1)
                            for kv in range(KVH):
                                idx = cnt_pK(s, kv)
                                slot = kslot(kv)
                                v.wait_ge(S["aKBE"], idx)
                                v.tensor_tensor(
                                    kT_own[:, kv * QL + s * SEG: kv * QL + (s + 1) * SEG],
                                    slot[:, :], krstd_bc[:, :], ALU.mult
                                ).then_inc(S["dK"], 1)
                            if is_own(s):
                                for j in range(SEGC):
                                    i = own_idx(s, j)
                                    v.wait_ge(S["aV2"], i + 1)
                                    if i > 0:
                                        v.wait_ge(S["aVR"], i)
                                    ins = v.tensor_reduce(
                                        ss_c[:, :],
                                        v2_bf[:, :].rearrange("p (kv d) -> p kv d", d=D),
                                        AX.X, ALU.add)
                                    fence(en, v, ins)
                                    v.tensor_scalar(ssm_c[:, :], ss_c[:, :], VN_EPS2,
                                                    None, ALU.max).then_inc(S["dSS"], 1)
                                    for kv in range(KVH):
                                        ii = i * KVH + kv
                                        v.wait_ge(S["pVT"], ii + 1)
                                        tq = (s - own_s0) * SEG + j * D
                                        last[0] = v.tensor_copy(
                                            vnT[:, kv * QL + tq: kv * QL + tq + D],
                                            pTR[:, :])
                                        last[0].then_inc(S["dVN"], 1)
                        v.drain().then_inc(S["bar1"], 1)

            esA = ExitStack()
            with esA:
                sbA = lambda nm, shp, dt: esA.enter_context(nc.sbuf_tensor(nm, shp, dt))
                yT = sbA("yT", [128, H * QL], BF16)
                esQT = ExitStack()
                esQT.__enter__()
                qT = esQT.enter_context(nc.sbuf_tensor("qT", [128, H * QL], BF16))

                esQ = ExitStack()
                with esQ:
                    wq_sl = [esQ.enter_context(nc.sbuf_tensor(f"wq{i}", [128, CC * OSW], BF16))
                             for i in range(2)]
                    q2_bf = esQ.enter_context(nc.sbuf_tensor("q2_bf", [128, QL], BF16))
                    qln_r = esQ.enter_context(nc.sbuf_tensor("qln_r", [1, QL], F32))
                    qrstd_r = esQ.enter_context(nc.sbuf_tensor("qrstd_r", [1, QL], F32))
                    qbc_bf = esQ.enter_context(nc.sbuf_tensor("qbc_bf", [128, QL], BF16))
                    qp = [esQ.enter_context(nc.psum_tensor(f"qp{i}", [128, QL], F32))
                          for i in range(2)]
                    qms = esQ.enter_context(nc.psum_tensor("qms", [1, QL], F32))
                    qbc = esQ.enter_context(nc.psum_tensor("qbc", [128, QL], F32))

                    with Block_() as blk:
                        @blk.gpsimd
                        def _(g):
                            g.wait_ge(S["bar1"], 4)
                            g.wait_ge(S["dGK"], 16)
                            g.wait_ge(S["dGV"], 16)
                            for si in range(NOS):
                                if si >= 2:
                                    g.wait_ge(S["pQ"], (si - 1) * HPS)
                                g.dma_start(
                                    wq_sl[si % 2][:, :].rearrange("p (cc o) -> p cc o", o=OSW),
                                    wq[:, si * OSW:(si + 1) * OSW].rearrange(
                                        "(cc p) o -> p cc o", p=128),
                                ).then_inc(S["dWQ0" if si % 2 == 0 else "dWQ1"], 16)
                            g.wait_ge(S["dRK"], 16 * NR)
                            g.wait_ge(S["dRV"], 16 * NR)
                            g.sem_inc(S["bar1g"], 1)

                        @blk.tensor
                        def _(t):
                            t.wait_ge(S["bar1"], 4)
                            for si in range(NOS):
                                for hh in range(HPS):
                                    h = si * HPS + hh
                                    if h == 0:
                                        t.wait_ge(S["dWQH"], 16)
                                    if h == max(si * HPS, 1):
                                        t.wait_ge(S["dWQ0" if si % 2 == 0 else "dWQ1"],
                                                  16 * (si // 2 + 1))
                                    slot = qp[h % 2]
                                    if h >= 2:
                                        t.wait_ge(S["dQ"], h - 1)
                                    for cc in range(CC):
                                        wsrc = (wqh[:, cc * D:(cc + 1) * D] if h == 0 else
                                                wq_sl[si % 2][:, cc * OSW + hh * D: cc * OSW + (hh + 1) * D])
                                        for (qo, qw) in QH:
                                            ins = t.matmul(
                                                slot[:, qo:qo + qw],
                                                wsrc,
                                                xnT[:, cc * QL + qo: cc * QL + qo + qw],
                                                start=(cc == 0), stop=(cc == CC - 1))
                                    ins.then_inc(S["pQ"], 1)
                                    t.wait_ge(S["aQ2"], h + 1)
                                    if h >= 1:
                                        t.wait_ge(S["aQR"], h)
                                    for (qo, qw) in QH:
                                        ins = t.matmul(qms[:1, qo:qo + qw], ones_bf[:, :],
                                                       q2_bf[:, qo:qo + qw],
                                                       start=True, stop=True)
                                    ins.then_inc(S["pQMS"], 1)
                                    t.wait_ge(S["aQR"], h + 1)
                                    if h >= 1:
                                        t.wait_ge(S["aQBE"], h)
                                    for (qo, qw) in QH:
                                        ins = t.matmul(qbc[:, qo:qo + qw], onesr_f[:1, :],
                                                       qrstd_r[:1, qo:qo + qw],
                                                       start=True, stop=True)
                                    ins.then_inc(S["pQBC"], 1)
                            t.nop().then_inc(S["bar2"], 1)

                        @blk.scalar
                        def _(a):
                            en = "scalar"
                            a.wait_ge(S["bar1"], 4)
                            a.wait_ge(S["dGK"], 16)
                            a.wait_ge(S["dGV"], 16)
                            for h in range(H):
                                slot = qp[h % 2]
                                a.wait_ge(S["pQ"], h + 1)
                                if h >= 1:
                                    a.wait_ge(S["pQMS"], h)
                                a.activation(q2_bf[:, :], slot[:, :], AF.Square
                                             ).then_inc(S["aQ2"], 1)
                                a.wait_ge(S["pQMS"], h + 1)
                                if h >= 1:
                                    a.wait_ge(S["pQBC"], h)
                                ins = a.activation(qln_r[:1, :], qms[:1, :], AF.Ln,
                                                   scale=1.0 / D, bias=eps_rms[:1, :])
                                fence(en, a, ins)
                                a.activation(qrstd_r[:1, :], qln_r[:1, :], AF.Exp,
                                             scale=-0.5).then_inc(S["aQR"], 1)
                                a.wait_ge(S["pQBC"], h + 1)
                                if h >= 1:
                                    a.wait_ge(S["dQ"], h)
                                ins = a.activation(qbc_bf[:, :], qbc[:, :], AF.Identity)
                                ins.then_inc(S["aQBE"], 1)
                            a.drain().then_inc(S["bar2"], 1)

                        @blk.vector
                        def _(v):
                            v.wait_ge(S["bar1"], 4)
                            v.wait_ge(S["dGK"], 16)
                            v.wait_ge(S["dGV"], 16)
                            for h in range(H):
                                slot = qp[h % 2]
                                v.wait_ge(S["aQBE"], h + 1)
                                ins = v.tensor_tensor(qT[:, h * QL:(h + 1) * QL],
                                                      slot[:, :], qbc_bf[:, :], ALU.mult)
                                ins.then_inc(S["dQ"], 1)
                            v.drain().then_inc(S["bar2"], 1)

                esAA = ExitStack()
                with esAA:
                    sbAA = lambda nm, shp, dt: esAA.enter_context(nc.sbuf_tensor(nm, shp, dt))
                    pT = [sbAA(f"pT{i}", [128, QL], BF16) for i in range(8)]
                    acc_d = sbAA("acc_d", [128, QL], F32)
                    if CC * D >= 2 * QL:
                        tmp_d = [wqh[:, 0:QL], wqh[:, QL:2 * QL]]
                    else:
                        tmp_d = [sbAA(f"tmp_d{i}", [128, QL], BF16)[:, :] for i in range(2)]
                    t_bf = sbAA("t_bf", [128, QL], BF16)
                    u_f = sbAA("u_f", [128, QL], F32)
                    r_r = sbAA("r_r", [1, QL], F32)
                    dr_r = sbAA("dr_r", [1, QL], BF16)
                    Sp = [esAA.enter_context(nc.psum_tensor(f"Sp{i}", [128, QL], F32))
                          for i in range(2)]
                    y_ps = esAA.enter_context(nc.psum_tensor("y_ps", [128, QL], F32))
                    TL = esAA.enter_context(nc.psum_tensor("TL", [128, QL], F32))
                    NPAIR = KCH // 2
                    USE_SLOTS = KCH >= 18 and (KCH // 2) >= 11
                    TSLOT = {5: 0, 9: 1, 13: 2, 17: 3} if USE_SLOTS else {}

                    with Block_() as blk:
                        @blk.tensor
                        def _(t):
                            t.wait_ge(S["bar2"], 3)
                            t.wait_ge(S["bar1g"], 1)

                            def tail_mm(t, hp, stage):
                                kv = hp // REP
                                if stage == 0:
                                    t.wait_ge(S["dA"], (hp + 1) * (KCH // 2))
                                    if hp >= 1:
                                        t.wait_ge(S["aR2"], 2 * hp)
                                    for (qo, qw) in QH:
                                        ins = t.matmul(TL[0:1, qo:qo + qw], ones_f[:, :],
                                                       acc_d[:, qo:qo + qw],
                                                       start=True, stop=True)
                                    ins.then_inc(S["pROW"], 1)
                                elif stage == 1:
                                    t.wait_ge(S["dT"], hp + 1)
                                    t.wait_ge(S["aR1"], hp + 1)
                                    for (qo, qw) in QH:
                                        ins = t.matmul(TL[0:1, qo:qo + qw], ones_bf[:, :],
                                                       t_bf[:, qo:qo + qw],
                                                       start=True, stop=True)
                                    ins.then_inc(S["pROW"], 1)
                                elif stage == 2:
                                    t.wait_ge(S["aR1"], hp + 1)
                                    t.wait_ge(S["dROWd"], hp + 1)
                                    for (qo, qw) in QH:
                                        ins = t.matmul(TL[:, qo:qo + qw], onesr_f[:1, :],
                                                       r_r[:1, qo:qo + qw],
                                                       start=True, stop=True)
                                    ins.then_inc(S["pBC2"], 1)
                                else:
                                    t.wait_ge(S["dROWd"], hp + 1)
                                    t.wait_ge(S["aR2"], 2 * hp + 1)
                                    for (qo, qw) in QH:
                                        ins = t.matmul(TL[:, qo:qo + qw], onesr_bf[:1, :],
                                                       dr_r[:1, qo:qo + qw],
                                                       start=True, stop=True)
                                    ins.then_inc(S["pBC2"], 1)

                            def emit_pv(t, h, j):
                                kv = h // REP
                                idx = h * KCH + j
                                t.wait_ge(S["aE"], idx + 1)
                                if j == 0 and h >= 1:
                                    t.wait_ge(S["dYE"], h)
                                for (qo, qw) in QH:
                                    ins = t.matmul(
                                        y_ps[:, qo:qo + qw],
                                        vT[:, j * KVD + kv * D: j * KVD + (kv + 1) * D],
                                        pT[idx % 8][:, qo:qo + qw],
                                        start=(j == 0), stop=(j == KCH - 1))
                                ins.then_inc(S["pPV"], 1)

                            for h in range(H):
                                kv = h // REP
                                if h >= 1 and not TSLOT:
                                    for st in range(4):
                                        tail_mm(t, h - 1, st)
                                for j in range(KCH):
                                    idx = h * KCH + j
                                    if idx >= 2:
                                        t.wait_ge(S["aE"], idx - 1)
                                    for (qo, qw) in QH:
                                        ins = t.matmul(
                                            Sp[j % 2][:, qo:qo + qw],
                                            kT[:, kv * NN + j * D: kv * NN + (j + 1) * D],
                                            qT[:, h * QL + qo: h * QL + qo + qw],
                                            start=True, stop=True)
                                    ins.then_inc(S["pS"], 1)
                                    if h >= 1 and j in TSLOT:
                                        tail_mm(t, h - 1, TSLOT[j])
                                    if j == 2:
                                        emit_pv(t, h, 0)
                                        emit_pv(t, h, 1)
                                    elif j >= 3:
                                        emit_pv(t, h, j - 1)
                                emit_pv(t, h, KCH - 1)
                            for st in range(4):
                                tail_mm(t, H - 1, st)
                            t.nop().then_inc(S["bar3"], 1)

                        @blk.gpsimd
                        def _(g):
                            for h in range(H):
                                for p in range(NPAIR):
                                    pg = h * NPAIR + p
                                    g.wait_ge(S["dP"], pg + 1)
                                    if pg >= 1:
                                        g.wait_ge(S["dA"], pg)
                                    if p == 0:
                                        if h >= 1:
                                            g.wait_ge(S["pROW"], 2 * h - 1)
                                        ins = g.tensor_copy(acc_d[:, :], tmp_d[pg % 2])
                                    else:
                                        ins = g.tensor_tensor(acc_d[:, :], acc_d[:, :],
                                                              tmp_d[pg % 2], ALU.add)
                                    ins.then_inc(S["dA"], 1)
                            g.drain().then_inc(S["bar3"], 1)

                        @blk.scalar
                        def _(a):
                            en = "scalar"
                            a.wait_ge(S["bar2"], 3)
                            a.wait_ge(S["bar1g"], 1)

                            def tail_act(a, hp, stage):
                                a.wait_ge(S["pROW"], 2 * hp + 1)
                                if hp >= 1:
                                    a.wait_ge(S["pBC2"], 2 * hp - 1)
                                    a.wait_ge(S["dROWd"], hp)
                                ins = a.activation(r_r[:1, :], TL[0:1, :], AF.Ln)
                                fence(en, a, ins)
                                a.activation(r_r[:1, :], r_r[:1, :], AF.Exp,
                                             scale=-1.0).then_inc(S["aR1"], 1)

                            ASLOT = {7: 0} if USE_SLOTS else {}
                            for h in range(H):
                                if h >= 1 and not ASLOT:
                                    tail_act(a, h - 1, 0)
                                for j in range(KCH):
                                    idx = h * KCH + j
                                    a.wait_ge(S["pS"], idx + 1)
                                    if idx >= 8:
                                        a.wait_ge(S["pPV"], idx - 7)
                                        a.wait_ge(S["dP"], (idx - 8) // 2 + 1)
                                    a.activation(pT[idx % 8][:, :], Sp[j % 2][:, :],
                                                 AF.Exp, scale=SCALE).then_inc(S["aE"], 1)
                                    if h >= 1 and j in ASLOT:
                                        tail_act(a, h - 1, ASLOT[j])
                                a.wait_ge(S["pPV"], (h + 1) * KCH)
                                if h >= 1:
                                    a.wait_ge(S["dY"], h)
                                a.activation(u_f[:, :], y_ps[:, :], AF.Identity
                                             ).then_inc(S["dYE"], 1)
                            tail_act(a, H - 1, 0)
                            a.drain().then_inc(S["bar3"], 1)

                        @blk.vector
                        def _(v):
                            en = "vector"
                            v.wait_ge(S["bar2"], 3)
                            v.wait_ge(S["bar1g"], 1)

                            def tail_dve(v, hp, stage):
                                kv = hp // REP
                                if stage == 0:
                                    v.wait_ge(S["pROW"], 2 * hp + 2)
                                    v.wait_ge(S["aR1"], hp + 1)
                                    if hp >= 1:
                                        v.wait_ge(S["pBC2"], 2 * hp)
                                    ins = v.tensor_tensor(dr_r[:1, :], TL[0:1, :],
                                                          r_r[:1, :], ALU.mult)
                                    ins.then_inc(S["dROWd"], 1)
                                elif stage == 1:
                                    v.wait_ge(S["pBC2"], 2 * hp + 1)
                                    ins = v.tensor_tensor(u_f[:, :], u_f[:, :],
                                                          TL[:, :], ALU.mult)
                                    ins.then_inc(S["aR2"], 1)
                                elif stage == 2:
                                    v.wait_ge(S["pBC2"], 2 * hp + 2)
                                    v.wait_ge(S["pROW"], 2 * hp + 2)
                                    ins = v.tensor_tensor(t_bf[:, :], TL[:, :],
                                                          vnT[:, kv * QL:(kv + 1) * QL],
                                                          ALU.mult)
                                    ins.then_inc(S["aR2"], 1)
                                else:
                                    v.wait_ge(S["aR2"], 2 * hp + 2)
                                    ins = v.tensor_tensor(yT[:, hp * QL:(hp + 1) * QL],
                                                          u_f[:, :], t_bf[:, :],
                                                          ALU.subtract)
                                    ins.then_inc(S["dY"], 1)

                            VSLOT = {4: 0, 7: 1, 9: 2, 10: 3} if USE_SLOTS else {}
                            NPAIR = KCH // 2
                            for h in range(H):
                                kv = h // REP
                                if h >= 1 and not VSLOT:
                                    for st in range(4):
                                        tail_dve(v, h - 1, st)
                                for p in range(NPAIR):
                                    j0 = 2 * p
                                    idx1 = h * KCH + j0 + 1
                                    pg = h * NPAIR + p
                                    v.wait_ge(S["aE"], idx1 + 1)
                                    if pg >= 2:
                                        v.wait_ge(S["dA"], pg - 1)
                                    g_idx = h * KCH + j0
                                    ins = v.tensor_tensor(tmp_d[pg % 2],
                                                          pT[(g_idx % 8)][:, :],
                                                          pT[(g_idx % 8) + 1][:, :], ALU.add)
                                    ins.then_inc(S["dP"], 1)
                                    if h >= 1 and p in VSLOT:
                                        tail_dve(v, h - 1, VSLOT[p])
                                v.wait_ge(S["dYE"], h + 1)
                                if h >= 1:
                                    v.wait_ge(S["pROW"], 2 * h)
                                    v.wait_ge(S["dY"], h)
                                ins = v.tensor_tensor(t_bf[:, :], u_f[:, :],
                                                      vnT[:, kv * QL:(kv + 1) * QL], ALU.mult)
                                ins.then_inc(S["dT"], 1)
                            for st in range(4):
                                tail_dve(v, H - 1, st)
                            v.drain().then_inc(S["bar3"], 1)

                esQT.__exit__(None, None, None)

                esW = ExitStack()
                with esW:
                    wo_sl = [esW.enter_context(nc.sbuf_tensor(f"wo{i}", [128, H * COLW], BF16))
                             for i in range(2)]
                    xr_b = [esW.enter_context(nc.sbuf_tensor(f"xr{i}", [128, COLW], F32))
                            for i in range(4)]
                    sc_b = [esW.enter_context(nc.sbuf_tensor(f"sc{i}", [128, COLW], F32))
                            for i in range(2)]
                    wps = [esW.enter_context(nc.psum_tensor(f"wp{i}", [128, COLW], F32))
                           for i in range(QT)]

                    with Block_() as blk:
                        @blk.gpsimd
                        def _(g):
                            g.wait_ge(S["pS"], H * KCH)
                            for cq in range(NCQ):
                                if cq >= 2:
                                    g.wait_ge(S["pW"], (cq - 1) * QT)
                                g.dma_start(
                                    wo_sl[cq % 2][:, :].rearrange("p (dc o) -> p dc o", o=COLW),
                                    wo[:, cq * COLW:(cq + 1) * COLW].rearrange(
                                        "(dc p) o -> p dc o", p=128),
                                ).then_inc(S["dWO0" if cq % 2 == 0 else "dWO1"], 16)

                        @blk.sync
                        def _(sy):
                            for cq in range(NCQ):
                                for qt in range(QT):
                                    i = cq * QT + qt
                                    if i < 4:
                                        sy.wait_ge(S["dY"], H)
                                    else:
                                        sy.wait_ge(S["dW"], i - 3)
                                    sy.dma_start(xr_b[i % 4][:, :],
                                                 xr[qt * D:(qt + 1) * D,
                                                    cq * COLW:(cq + 1) * COLW]
                                                 ).then_inc(S[f"dXR{i % 4}"], 16)

                        @blk.scalar
                        def _(a):
                            for cq in range(NCQ):
                                for qt in range(QT):
                                    i = cq * QT + qt
                                    a.wait_ge(S["dW"], i + 1)
                                    a.dma_start(scr1[qt * D:(qt + 1) * D,
                                                     cq * COLW:(cq + 1) * COLW],
                                                sc_b[i % 2][:, :]).then_inc(S["dSC0" if i % 2 == 0 else "dSC1"], 16)

                        @blk.tensor
                        def _(t):
                            t.wait_ge(S["dY"], H)
                            for cq in range(NCQ):
                                t.wait_ge(S["dWO0" if cq % 2 == 0 else "dWO1"], 16 * (cq // 2 + 1))
                                for qt in range(QT):
                                    if cq >= 1:
                                        t.wait_ge(S["dW"], (cq - 1) * QT + qt + 1)
                                    for dc in range(H):
                                        ins = t.matmul(
                                            wps[qt][:, :COLW],
                                            yT[:, dc * QL + qt * D: dc * QL + (qt + 1) * D],
                                            wo_sl[cq % 2][:, dc * COLW:(dc + 1) * COLW],
                                            start=(dc == 0), stop=(dc == H - 1))
                                    ins.then_inc(S["pW"], 1)

                        @blk.vector
                        def _(v):
                            v.wait_ge(S["dY"], H)
                            for cq in range(NCQ):
                                for qt in range(QT):
                                    i = cq * QT + qt
                                    v.wait_ge(S["pW"], i + 1)
                                    v.wait_ge(S[f"dXR{i % 4}"], 16 * (i // 4 + 1))
                                    if i >= 2:
                                        v.wait_ge(S["dSC0" if i % 2 == 0 else "dSC1"], 16 * (i // 2))
                                    ins = v.tensor_tensor(sc_b[i % 2][:, :],
                                                          wps[qt][:, :COLW],
                                                          xr_b[i % 4][:, :], ALU.add)
                                    ins.then_inc(S["dW"], 1)
                            v.drain().then_inc(S["bar4"], 1)

        esF = ExitStack()
        with esF:
            sbF = lambda nm, shp, dt: esF.enter_context(nc.sbuf_tensor(nm, shp, dt))
            gatedT = sbF("gatedT", [128, FFC * QL], BF16)
            silu_bf = sbF("silu_bf", [128, QL], BF16)
            wg_sl = [sbF(f"wg{i}", [128, CC * HSW], BF16) for i in range(2)]
            wu_sl = [sbF(f"wu{i}", [128, CC * HSW], BF16) for i in range(2)]
            wd_sl = [sbF(f"wd{i}", [128, GH * COLW], BF16) for i in range(2)]
            sr_b = [sbF(f"sr{i}", [128, COLW], F32) for i in range(4)]
            ou_b = [sbF(f"ou{i}", [128, COLW], F32) for i in range(2)]

            wd_done = set()

            def wd_dma(g, i):
                cq, hg = divmod(i, NG)
                g.dma_start(
                    wd_sl[i % 2][:, :].rearrange("p (hc o) -> p hc o", o=COLW),
                    wd[hg * GH * D:(hg + 1) * GH * D,
                       cq * COLW:(cq + 1) * COLW].rearrange(
                        "(hc p) o -> p hc o", p=128)
                ).then_inc(S["dWD0" if i % 2 == 0 else "dWD1"], 16)

            esG = ExitStack()
            with esG:
                g_ps = [esG.enter_context(nc.psum_tensor(f"g{i}", [128, QL], F32))
                        for i in range(2)]
                u_ps = [esG.enter_context(nc.psum_tensor(f"u{i}", [128, QL], F32))
                        for i in range(2)]

                with Block_() as blk:
                    @blk.gpsimd
                    def _(g):
                        NQ_ = NCQ * QT
                        for hs in range(NHS):
                            if hs == 0:
                                g.wait_ge(S["pW"], NQ_)
                            elif hs >= 2:
                                g.wait_ge(S["pGU"], (hs - 1) * HPG)
                            g.dma_start(
                                wg_sl[hs % 2][:, :].rearrange("p (cc o) -> p cc o", o=HSW),
                                wg[:, hs * HSW:(hs + 1) * HSW].rearrange(
                                    "(cc p) o -> p cc o", p=128)).then_inc(
                                S["dWG0" if hs % 2 == 0 else "dWG1"], 16)
                            if hs == 0:
                                g.wait_ge(S["dW"], NQ_)
                            g.dma_start(
                                wu_sl[hs % 2][:, :].rearrange("p (cc o) -> p cc o", o=HSW),
                                wu[:, hs * HSW:(hs + 1) * HSW].rearrange(
                                    "(cc p) o -> p cc o", p=128)).then_inc(
                                S["dWG0" if hs % 2 == 0 else "dWG1"], 16)
                            if hs == 1 and NCQ * NG >= 1:
                                g.wait_ge(S["dSC0"], 16 * ((NQ_ + 1) // 2))
                                g.wait_ge(S["dSC1"], 16 * (NQ_ // 2))
                                wd_done.add(0)
                                wd_dma(g, 0)
                            elif hs == 3 and NCQ * NG >= 2:
                                wd_done.add(1)
                                wd_dma(g, 1)

                    @blk.tensor
                    def _(t):
                        t.wait_ge(S["bar4"], 1)
                        for hs in range(NHS):
                            t.wait_ge(S["dWG0" if hs % 2 == 0 else "dWG1"], 32 * (hs // 2 + 1))
                            for hh in range(HPG):
                                hc = hs * HPG + hh
                                p = hc % 2
                                if hc >= 2:
                                    t.wait_ge(S["dGT"], hc - 1)
                                for cc in range(CC):
                                    for (qo, qw) in QH:
                                        t.matmul(
                                            g_ps[p][:, qo:qo + qw],
                                            wg_sl[hs % 2][:, cc * HSW + hh * D: cc * HSW + (hh + 1) * D],
                                            xnT[:, cc * QL + qo: cc * QL + qo + qw],
                                            start=(cc == 0), stop=(cc == CC - 1))
                                for cc in range(CC):
                                    for (qo, qw) in QH:
                                        ins = t.matmul(
                                            u_ps[p][:, qo:qo + qw],
                                            wu_sl[hs % 2][:, cc * HSW + hh * D: cc * HSW + (hh + 1) * D],
                                            xnT[:, cc * QL + qo: cc * QL + qo + qw],
                                            start=(cc == 0), stop=(cc == CC - 1))
                                ins.then_inc(S["pGU"], 1)

                    @blk.scalar
                    def _(a):
                        for hc in range(FFC):
                            p = hc % 2
                            a.wait_ge(S["pGU"], hc + 1)
                            if hc >= 1:
                                a.wait_ge(S["dGT"], hc)
                            ins = a.activation(silu_bf[:, :], g_ps[p][:, :], AF.Silu)
                            ins.then_inc(S["aSIL"], 1)

                    @blk.vector
                    def _(v):
                        for hc in range(FFC):
                            p = hc % 2
                            v.wait_ge(S["aSIL"], hc + 1)
                            ins = v.tensor_tensor(gatedT[:, hc * QL:(hc + 1) * QL],
                                                  u_ps[p][:, :], silu_bf[:, :], ALU.mult)
                            ins.then_inc(S["dGT"], 1)

            esD = ExitStack()
            with esD:
                dps = [esD.enter_context(nc.psum_tensor(f"d{i}", [128, COLW], F32))
                       for i in range(QT)]

                with Block_() as blk:
                    @blk.gpsimd
                    def _(g):
                        NQ_ = NCQ * QT
                        for i in (0, 1):
                            if i not in wd_done and i < NCQ * NG:
                                g.wait_ge(S["dSC0"], 16 * ((NQ_ + 1) // 2))
                                g.wait_ge(S["dSC1"], 16 * (NQ_ // 2))
                                wd_dma(g, i)
                        for i in range(2, NCQ * NG):
                            cq2, hg2 = divmod(i - 2, NG)
                            if hg2 == NG - 1:
                                g.wait_ge(S["pDN"], cq2 * QT + QT)
                            else:
                                g.wait_ge(S["fWD"], sum(
                                    1 for k in range(i - 1) if k % NG != NG - 1))
                            wd_dma(g, i)

                    @blk.sync
                    def _(sy):
                        NQ_ = NCQ * QT
                        sy.wait_ge(S["dSC0"], 16 * ((NQ_ + 1) // 2))
                        sy.wait_ge(S["dSC1"], 16 * (NQ_ // 2))
                        for cq in range(NCQ):
                            for qt in range(QT):
                                i = cq * QT + qt
                                if i >= 4:
                                    sy.wait_ge(S["dF"], i - 3)
                                sy.dma_start(sr_b[i % 4][:, :],
                                             scr1[qt * D:(qt + 1) * D,
                                                  cq * COLW:(cq + 1) * COLW]
                                             ).then_inc(S[f"dSC2{i % 4}"], 16)

                    @blk.scalar
                    def _(a):
                        NQ_ = NCQ * QT
                        for cq in range(NCQ):
                            for qt in range(QT):
                                i = cq * QT + qt
                                a.wait_ge(S["dF"], i + 1)
                                a.dma_start(out[qt * D:(qt + 1) * D,
                                                cq * COLW:(cq + 1) * COLW],
                                            ou_b[i % 2][:, :]).then_inc(
                                    S["dOUT0" if i % 2 == 0 else "dOUT1"], 16)
                        a.wait_ge(S["dOUT0"], 16 * ((NQ_ + 1) // 2))
                        a.wait_ge(S["dOUT1"], 16 * (NQ_ // 2))

                    @blk.tensor
                    def _(t):
                        t.wait_ge(S["dGT"], FFC)
                        t.wait_ge(S["aSIL"], FFC)
                        for cq in range(NCQ):
                            for hg in range(NG):
                                i = cq * NG + hg
                                t.wait_ge(S["dWD0" if i % 2 == 0 else "dWD1"], 16 * (i // 2 + 1))
                                if hg != NG - 1:
                                    for h16 in range(GH):
                                        h = hg * GH + h16
                                        for qt in range(QT):
                                            if h == 0 and cq >= 1:
                                                t.wait_ge(S["dF"], (cq - 1) * QT + qt + 1)
                                            ins = t.matmul(
                                                dps[qt][:, :COLW],
                                                gatedT[:, h * QL + qt * D: h * QL + (qt + 1) * D],
                                                wd_sl[i % 2][:, h16 * COLW:(h16 + 1) * COLW],
                                                start=(h == 0), stop=False)
                                    ins.then_inc(S["fWD"], 1)
                                else:
                                    for qt in range(QT):
                                        for h16 in range(GH):
                                            h = hg * GH + h16
                                            if h == 0 and cq >= 1:
                                                t.wait_ge(S["dF"], (cq - 1) * QT + qt + 1)
                                            ins = t.matmul(
                                                dps[qt][:, :COLW],
                                                gatedT[:, h * QL + qt * D: h * QL + (qt + 1) * D],
                                                wd_sl[i % 2][:, h16 * COLW:(h16 + 1) * COLW],
                                                start=(h == 0), stop=(h == FFC - 1))
                                            if h == FFC - 1:
                                                ins.then_inc(S["pDN"], 1)

                    @blk.vector
                    def _(v):
                        for cq in range(NCQ):
                            for qt in range(QT):
                                i = cq * QT + qt
                                v.wait_ge(S["pDN"], i + 1)
                                v.wait_ge(S[f"dSC2{i % 4}"], 16 * (i // 4 + 1))
                                if i >= 2:
                                    v.wait_ge(S["dOUT0" if i % 2 == 0 else "dOUT1"], 16 * (i // 2))
                                ins = v.tensor_tensor(ou_b[i % 2][:, :],
                                                      dps[qt][:, :COLW],
                                                      sr_b[i % 4][:, :], ALU.add)
                                ins.then_inc(S["dF"], 1)

    return nc






from concourse.bass_utils import run_bass_kernel_spmd

_C, _NN, _QL, _FF, _H, _KVH, _SEG = 2048, 4096, 1024, 8192, 16, 4, 512
_NB, _NCORE = 2, 8
_CPB = _NCORE // _NB


def kernel(L0_attn_x, L0_attn_Wq, L0_attn_Wk, L0_attn_Wv, L0_attn_Wo,
           L0_ffn_w_gate, L0_ffn_w_up, L0_ffn_w_down):
    x = np.asarray(L0_attn_x, dtype=np.float32)
    w = {
        "wq": np.ascontiguousarray(np.asarray(L0_attn_Wq, np.float32).T),
        "wk": np.ascontiguousarray(np.asarray(L0_attn_Wk, np.float32).T),
        "wv": np.ascontiguousarray(np.asarray(L0_attn_Wv, np.float32).T),
        "wo": np.ascontiguousarray(np.asarray(L0_attn_Wo, np.float32).T),
        "wg": np.ascontiguousarray(np.asarray(L0_ffn_w_gate, np.float32).T),
        "wu": np.ascontiguousarray(np.asarray(L0_ffn_w_up, np.float32).T),
        "wd": np.ascontiguousarray(np.asarray(L0_ffn_w_down, np.float32).T),
    }
    in_maps = []
    for c in range(_NCORE):
        b, r = divmod(c, _CPB)
        q0 = r * _QL
        m = {"xT": np.ascontiguousarray(x[b, q0:q0 + _QL].T),
             "xr": np.ascontiguousarray(x[b, q0:q0 + _QL])}
        m.update(w)
        in_maps.append(m)
    nc = build_core(_C, _NN, _QL, 0, _FF, _H, _KVH, _SEG,
                    NR=_CPB, NCORES=_NCORE)
    res = run_bass_kernel_spmd(nc, in_maps, core_ids=list(range(_NCORE)))
    out = np.empty((_NB, _NN, _C), dtype=np.float32)
    for c in range(_NCORE):
        b, r = divmod(c, _CPB)
        out[b, r * _QL:(r + 1) * _QL, :] = res.results[c]["out"]
    return out


# version 36
# speedup vs baseline: 1.0510x; 1.0510x over previous
from contextlib import ExitStack
import numpy as np
import concourse.bass as bass
import concourse.mybir as mybir

F32 = mybir.dt.float32
BF16 = mybir.dt.bfloat16
AF = mybir.ActivationFunctionType
ALU = mybir.AluOpType
AX = mybir.AxisListType

D = 128
LN_EPS = 1e-5
RMS_EPS = float(np.finfo(np.float32).eps)
VN_EPS2 = 1e-24


def halves(total, w=512):
    return [(o, min(w, total - o)) for o in range(0, total, w)]


def build_core(C, NN, QL, q0, FF, H, KVH, SEG=512, GHmax=8, NR=1, NCORES=1):
    CC = C // D
    SEGN = NN // SEG
    SEGC = SEG // D
    KCH = NN // D
    QT = QL // D
    FFC = FF // D
    KVD = KVH * D
    REP = H // KVH
    QH = halves(QL)
    OSW = min(256, H * D)
    NOS = (H * D) // OSW
    HPS = OSW // D
    HSW = min(128, FF)
    NHS = FF // HSW
    HPG = HSW // D
    GH = min(GHmax, FFC)
    NG = FFC // GH
    CH = halves(C)
    COLW = CH[0][1]
    NCQ = len(CH)
    own_s0 = 0
    own_sn = QL // SEG
    PSEGN = QL // SEG
    assert q0 == 0 and NN == NR * QL
    SCALE = float(1.0 / np.sqrt(D))
    assert KVH % 2 == 0 and SEGC % 2 == 0 and QT <= 8

    nc = bass.Bass("TRN2", target_bir_lowering=False)
    xT = nc.dram_tensor("xT", [C, QL], F32, kind="ExternalInput").ap()
    xr = nc.dram_tensor("xr", [QL, C], F32, kind="ExternalInput").ap()
    wq = nc.dram_tensor("wq", [C, H * D], F32, kind="ExternalInput").ap()
    wk = nc.dram_tensor("wk", [C, KVD], F32, kind="ExternalInput").ap()
    wv = nc.dram_tensor("wv", [C, KVD], F32, kind="ExternalInput").ap()
    wo = nc.dram_tensor("wo", [H * D, C], F32, kind="ExternalInput").ap()
    wg = nc.dram_tensor("wg", [C, FF], F32, kind="ExternalInput").ap()
    wu = nc.dram_tensor("wu", [C, FF], F32, kind="ExternalInput").ap()
    wd = nc.dram_tensor("wd", [FF, C], F32, kind="ExternalInput").ap()
    out = nc.dram_tensor("out", [QL, C], F32, kind="ExternalOutput").ap()
    scr1 = nc.dram_tensor("scr1", [QL, C], F32, kind="Internal").ap()
    st_k = nc.dram_tensor("st_k", [KVH * D, QL], BF16, kind="Internal").ap()
    st_v = nc.dram_tensor("st_v", [(QL // D) * D, KVD], BF16, kind="Internal").ap()
    G_k = nc.dram_tensor("G_k", [NR * KVH * D, QL], BF16, kind="Internal").ap()
    G_v = nc.dram_tensor("G_v", [NR * QL, KVD], BF16, kind="Internal").ap()

    es = ExitStack()
    with es:
        sb = lambda nm, shp, dt: es.enter_context(nc.sbuf_tensor(nm, shp, dt))
        sem = lambda nm: es.enter_context(nc.semaphore(nm))
        xnT = sb("xnT", [128, CC * QL], BF16)
        ones_bf = sb("ones_bf", [128, 1], BF16)
        ones_f = sb("ones_f", [128, 1], F32)
        onesr_f = sb("onesr_f", [1, 128], F32)
        onesr_bf = sb("onesr_bf", [1, 128], BF16)
        ident = sb("ident", [128, 128], BF16)
        eps_ln = sb("eps_ln", [1, 1], F32)
        eps_rms = sb("eps_rms", [1, 1], F32)

        names = ("gI dWKV dXA0 dXA1 aX2 pSTh dGK dGV ccK ccV dRK dRV dVAR aRSTD dMR pBC aBCE dAPP pAPP "
                 "pK aK2 pKMS aKR pKBC aKBE dK pV aV "
                 "aV2 dSS aVR aVS pVT dVN "
                 "dWQH dWQ0 dWQ1 fWQ pQ aQ2 pQMS aQR pQBC aQBE dQ "
                 "pS aE pPV dA dP gQ dT dYE pROW dROWd aR1 pBC2 aR2 dY "
                 "dWO0 dWO1 pW dXR0 dXR1 dXR2 dXR3 dW dSC0 dSC1 "
                 "dWG0 dWG1 fWG pGU aSIL dGT dWD0 dWD1 fWD pDN dSC20 dSC21 dSC22 dSC23 dF dOUT0 dOUT1 "
                 "bar1 bar1g bar2 bar3 bar4 "
                 "slfT slfS slfV slfG slfY").split()
        S = {n: sem(n) for n in names}
        slf_cnt = {"tensor": 0, "scalar": 0, "vector": 0, "gpsimd": 0, "sync": 0}
        slf_sem = {"tensor": S["slfT"], "scalar": S["slfS"], "vector": S["slfV"],
                   "gpsimd": S["slfG"], "sync": S["slfY"]}

        def fence(en, e, ins):
            slf_cnt[en] += 1
            ins.then_inc(slf_sem[en], 1)
            e.wait_ge(slf_sem[en], slf_cnt[en])

        import contextlib
        _outer_blk = es.enter_context(nc.Block())

        def Block_():
            return contextlib.nullcontext(_outer_blk)

        def cnt_pK(s, kv):
            return s * KVH + kv + 1

        def cnt_pV(s, j):
            return s * SEGC + j + 1

        def is_own(s):
            return own_s0 <= s < own_s0 + own_sn

        def own_idx(s, j):
            return (s - own_s0) * SEGC + j


        with Block_() as blk:
            @blk.gpsimd
            def _(g):
                g.memset(ones_bf[:, :], 1.0).then_inc(S["gI"], 1)
                g.memset(ones_f[:, :], 1.0).then_inc(S["gI"], 1)
                g.memset(onesr_f[:1, :], 1.0).then_inc(S["gI"], 1)
                g.memset(onesr_bf[:1, :], 1.0).then_inc(S["gI"], 1)
                g.memset(ident[:, :], 0.0).then_inc(S["gI"], 1)
                g.memset(eps_ln[:1, :], LN_EPS).then_inc(S["gI"], 1)
                g.memset(eps_rms[:1, :], RMS_EPS).then_inc(S["gI"], 1)
                g.wait_ge(S["gI"], 7)
                g.affine_select(out=ident[:, :], in_=ident[:, :],
                                compare_op=ALU.not_equal, fill=1.0, base=0,
                                pattern=[[-1, 128]], channel_multiplier=1
                                ).then_inc(S["gI"], 1)

        es2 = ExitStack()
        with es2:
            sb2 = lambda nm, shp, dt: es2.enter_context(nc.sbuf_tensor(nm, shp, dt))
            kT = sb2("kT", [128, KVH * NN], BF16)
            vT = sb2("vT", [128, KCH * KVD], BF16)
            vnT = sb2("vnT", [128, KVH * QL], BF16)
            wqh = sb2("wqh", [128, CC * D], BF16)

            esP = ExitStack()
            with esP:
                sbP = lambda nm, shp, dt: esP.enter_context(nc.sbuf_tensor(nm, shp, dt))
                psP = lambda nm, shp, dt=F32: esP.enter_context(nc.psum_tensor(nm, shp, dt))
                xA = [sbP("xA0", [128, CC * SEG], BF16)]
                x2h = sbP("x2h", [128, (CC // 2) * SEG], BF16)
                kT_own = sbP("kT_own", [128, KVH * QL], BF16)
                vT_own = sbP("vT_own", [128, (QL // D) * KVD], BF16)
                wk_bf = sbP("wk_bf", [128, CC * KVD], BF16)
                wv_bf = sbP("wv_bf", [128, CC * KVD], BF16)
                mean_r = sbP("mean_r", [1, SEG], F32)
                ms_r = sbP("ms_r", [1, SEG], F32)
                var_r = sbP("var_r", [1, SEG], F32)
                lnv_r = sbP("lnv_r", [1, SEG], F32)
                rstd_r = sbP("rstd_r", [1, SEG], F32)
                mrstd_r = sbP("mrstd_r", [1, SEG], F32)
                rstd_bc = sbP("rstd_bc", [128, SEG], BF16)
                mrstd_bc = sbP("mrstd_bc", [128, SEG], BF16)
                tmp_bf = sbP("tmp_bf", [128, SEG], BF16)
                k2_bf = sbP("k2_bf", [128, SEG], BF16)
                klnr = sbP("klnr", [1, SEG], F32)
                krstd_r = sbP("krstd_r", [1, SEG], F32)
                krstd_bc = sbP("krstd_bc", [128, SEG], BF16)
                v2_bf = sbP("v2_bf", [128, KVD], BF16)
                ss_c = sbP("ss_c", [128, KVH], F32)
                ssm_c = sbP("ssm_c", [128, KVH], F32)
                sln_c = sbP("sln_c", [128, KVH], F32)
                vrstd_c = sbP("vrstd_c", [128, KVH], F32)
                vn_buf = sbP("vn_buf", [128, KVH * D], BF16)
                pA = psP("pA", [1, SEG])
                pB = psP("pB", [1, SEG])
                pC = psP("pC", [128, SEG])
                pE = psP("pE", [128, SEG])
                pF = psP("pF", [128, SEG])
                pD = psP("pD", [128, SEG])
                pG = psP("pG", [1, SEG])
                pTR = psP("pTR", [128, D], BF16)

                def xnT_src(s):
                    return xnT, (s - own_s0) * SEG, QL

                def xnt_ap(s, cc):
                    xnt, xb, xstr = xnT_src(s)
                    return xnt[:, xb + cc * xstr: xb + cc * xstr + SEG]

                def xnt_stat_ap(s, cc, j):
                    xnt, xb, xstr = xnT_src(s)
                    return xnt[:, xb + cc * xstr + j * D: xb + cc * xstr + (j + 1) * D]

                def xsrc(s, cc):
                    if s == 0:
                        return xA[0][:, cc * SEG:(cc + 1) * SEG]
                    return xnt_ap(s, cc)

                def xsrc_half(s, half):
                    HCC = CC // 2
                    if s == 0:
                        return xA[0][:, half * HCC * SEG:(half + 1) * HCC * SEG]
                    xnt, xb, xstr = xnT_src(s)
                    return xnt.rearrange("p (cc q) -> p cc q", q=xstr)[
                        :, half * HCC:(half + 1) * HCC, xb:xb + SEG]

                def vslot(s, j):
                    return pE if (KVH + j) % 2 == 0 else pF

                def kslot(kv):
                    return pE if kv % 2 == 0 else pF

                with Block_() as blk:
                    POOL_CC = CC // 4
                    DVE_CC = CC - POOL_CC

                    @blk.gpsimd
                    def _(g):
                        en = "gpsimd"
                        g.dma_start(
                            xA[0][:, :].rearrange("p (cc n) -> p cc n", n=SEG),
                            xT[:, 0:SEG].rearrange("(cc p) n -> p cc n", p=128),
                        ).then_inc(S["dXA0"], 16)
                        g.dma_start(wk_bf[:, :].rearrange("p (cc o) -> p cc o", o=KVD),
                                    wk.rearrange("(cc p) o -> p cc o", p=128)
                                    ).then_inc(S["dWKV"], 16)
                        g.dma_start(wv_bf[:, :].rearrange("p (cc o) -> p cc o", o=KVD),
                                    wv.rearrange("(cc p) o -> p cc o", p=128)
                                    ).then_inc(S["dWKV"], 16)
                        for s2 in range(1, PSEGN):
                            xnt2, xb2, xstr2 = xnT_src(s2)
                            g.dma_start(
                                xnt2.rearrange("p (cc q) -> p cc q", q=xstr2)[
                                    :, :, xb2:xb2 + SEG],
                                xT[:, s2 * SEG:(s2 + 1) * SEG].rearrange(
                                    "(cc p) n -> p cc n", p=128),
                            ).then_inc(S["dXA1"], 16)
                        for s in range(PSEGN):
                            xnt, xb, xstr = xnT_src(s)
                            if POOL_CC:
                                g.wait_ge(S["pSTh"], 2 * s + 2)
                                g.wait_ge(S["aBCE"], 2 * s + 2)
                                for ci in range(POOL_CC):
                                    cc = DVE_CC + ci
                                    ins = g.tensor_tensor(
                                        x2h[:, 0:SEG], xsrc(s, cc),
                                        rstd_bc[:, :], ALU.mult)
                                    fence(en, g, ins)
                                    ins = g.tensor_tensor(
                                        xnt[:, xb + cc * xstr: xb + cc * xstr + SEG],
                                        x2h[:, 0:SEG], mrstd_bc[:, :], ALU.subtract)
                                    if ci == POOL_CC - 1:
                                        ins.then_inc(S["pAPP"], 1)
                                    else:
                                        fence(en, g, ins)
                            else:
                                g.sem_inc(S["pAPP"], 1)
                        g.sem_inc(S["bar1"], 1)
                        g.dma_start(
                            wqh[:, :].rearrange("p (cc o) -> p cc o", o=D),
                            wq[:, 0:D].rearrange("(cc p) o -> p cc o", p=128),
                        ).then_inc(S["dWQH"], 16)
                        g.wait_ge(S["dK"], PSEGN * KVH)
                        g.wait_ge(S["aV"], PSEGN * SEGC)
                        g.dma_start(st_k.rearrange("(kv d) t -> d kv t", d=128),
                                    kT_own[:, :].rearrange("d (kv t) -> d kv t", t=QL)
                                    ).then_inc(S["dGK"], 16)
                        g.dma_start(st_v.rearrange("(j p) o -> p j o", p=128),
                                    vT_own[:, :].rearrange("p (j o) -> p j o", o=KVD)
                                    ).then_inc(S["dGV"], 16)
                        g.wait_ge(S["dGK"], 16)
                        g.wait_ge(S["dGV"], 16)
                        if NR > 1:
                            groups = [[gr * NR + i for i in range(NR)]
                                      for gr in range(NCORES // NR)]
                            g.collective_compute(
                                "AllGather", ALU.bypass, replica_groups=groups,
                                ins=[st_k.opt()], outs=[G_k.opt()],
                            ).then_inc(S["ccK"], 1)
                            g.collective_compute(
                                "AllGather", ALU.bypass, replica_groups=groups,
                                ins=[st_v.opt()], outs=[G_v.opt()],
                            ).then_inc(S["ccV"], 1)
                            gk_src, gv_src = G_k, G_v
                        else:
                            gk_src, gv_src = st_k, st_v

                    @blk.sync
                    def _(sy):
                        if NR > 1:
                            sy.wait_ge(S["ccK"], 1)
                            sy.wait_ge(S["ccV"], 1)
                        else:
                            sy.wait_ge(S["dGK"], 16)
                            sy.wait_ge(S["dGV"], 16)
                        gks = G_k if NR > 1 else st_k
                        gvs = G_v if NR > 1 else st_v
                        QT_ = QL // D
                        for r in range(NR):
                            sy.dma_start(
                                kT[:, :].rearrange("d (kv n) -> d kv n", n=NN)[
                                    :, :, r * QL:(r + 1) * QL],
                                gks[r * KVH * D:(r + 1) * KVH * D, :].rearrange(
                                    "(kv d) t -> d kv t", d=128)
                            ).then_inc(S["dRK"], 16)
                            sy.dma_start(
                                vT[:, (r * QT_) * KVD:((r + 1) * QT_) * KVD].rearrange(
                                    "p (j o) -> p j o", o=KVD),
                                gvs[r * QL:(r + 1) * QL, :].rearrange(
                                    "(j p) o -> p j o", p=128)
                            ).then_inc(S["dRV"], 16)


                    @blk.tensor
                    def _(t):
                        t.wait_ge(S["gI"], 8)
                        last = [None]
                        def _stats(s):
                            if s == 0:
                                t.wait_ge(S["dXA0"], 16)
                            else:
                                t.wait_ge(S["dXA1"], 16 * s)
                                t.wait_ge(S["dVAR"], s)
                            for cc in range(CC):
                                t.matmul(pA[:1, :], ones_bf[:, :],
                                         xsrc(s, cc),
                                         start=(cc == 0), stop=(cc == CC - 1))
                            HCC = CC // 2
                            for half in range(2):
                                t.wait_ge(S["aX2"], 2 * s + half + 1)
                                for c2 in range(HCC):
                                    cc = half * HCC + c2
                                    ins = t.matmul(pB[:1, :], ones_bf[:, :],
                                                   x2h[:, c2 * SEG:(c2 + 1) * SEG],
                                                   start=(cc == 0), stop=(cc == CC - 1))
                                ins.then_inc(S["pSTh"], 1)

                        def _bcasts(s):
                            t.wait_ge(S["aRSTD"], s + 1)
                            if s >= 1:
                                t.wait_ge(S["aBCE"], 2 * s)
                            t.matmul(pC[:, :], onesr_f[:1, :], rstd_r[:1, :],
                                     start=True, stop=True).then_inc(S["pBC"], 1)
                            t.wait_ge(S["dMR"], s + 1)
                            t.wait_ge(S["aBCE"], 2 * s + 1)
                            t.matmul(pC[:, :], onesr_f[:1, :], mrstd_r[:1, :],
                                     start=True, stop=True).then_inc(S["pBC"], 1)

                        def _projs(s):
                            if s == 0:
                                t.wait_ge(S["dWKV"], 32)
                            for kv in range(KVH):
                                idx = cnt_pK(s, kv)
                                t.wait_ge(S["dAPP"], s + 1)
                                t.wait_ge(S["pAPP"], s + 1)
                                slot = kslot(kv)
                                if kv >= 2:
                                    t.wait_ge(S["dK"], idx - 2)
                                elif s >= 1:
                                    t.wait_ge(S["aV"], cnt_pV(s - 1, SEGC - 2 + kv))
                                for cc in range(CC):
                                    ins = t.matmul(
                                        slot[:, :],
                                        wk_bf[:, cc * KVD + kv * D: cc * KVD + (kv + 1) * D],
                                        xnt_ap(s, cc),
                                        start=(cc == 0), stop=(cc == CC - 1))
                                ins.then_inc(S["pK"], 1)
                                t.wait_ge(S["aK2"], idx)
                                if idx > 1:
                                    t.wait_ge(S["aKR"], idx - 1)
                                t.matmul(pG[:1, :], ones_bf[:, :], k2_bf[:, :],
                                         start=True, stop=True).then_inc(S["pKMS"], 1)
                                t.wait_ge(S["aKR"], idx)
                                if idx > 1:
                                    t.wait_ge(S["aKBE"], idx - 1)
                                t.matmul(pD[:, :], onesr_f[:1, :], krstd_r[:1, :],
                                         start=True, stop=True).then_inc(S["pKBC"], 1)
                            for j in range(SEGC):
                                jdx = cnt_pV(s, j)
                                slot = vslot(s, j)
                                if j < 2:
                                    t.wait_ge(S["dK"], s * KVH + (KVH - 2 + j) + 1)
                                else:
                                    t.wait_ge(S["aV"], jdx - 2)
                                for cc in range(CC):
                                    ins = t.matmul(
                                        slot[:, :KVD],
                                        xnt_stat_ap(s, cc, j),
                                        wv_bf[:, cc * KVD: (cc + 1) * KVD],
                                        start=(cc == 0), stop=(cc == CC - 1))
                                ins.then_inc(S["pV"], 1)
                            if is_own(s):
                                for j in range(SEGC):
                                    for kv in range(KVH):
                                        ii = own_idx(s, j) * KVH + kv
                                        t.wait_ge(S["aVS"], ii + 1)
                                        if ii > 0:
                                            t.wait_ge(S["dVN"], ii)
                                        last[0] = t.transpose(
                                            pTR[:, :], vn_buf[:, kv * D:(kv + 1) * D],
                                            ident[:, :])
                                        last[0].then_inc(S["pVT"], 1)

                        for s in range(PSEGN):
                            _stats(s)
                            if s >= 1:
                                _projs(s - 1)
                            _bcasts(s)
                        _projs(PSEGN - 1)
                        t.nop().then_inc(S["bar1"], 1)

                    @blk.scalar
                    def _(a):
                        en = "scalar"
                        last = [None]
                        a.wait_ge(S["gI"], 8)
                        def _sq(s):
                            if s == 0:
                                a.wait_ge(S["dXA0"], 16)
                            else:
                                a.wait_ge(S["dXA1"], 16 * s)
                                a.wait_ge(S["pAPP"], s)
                            for half in range(2):
                                if 2 * s + half > 0:
                                    a.wait_ge(S["pSTh"], 2 * s + half)
                                a.activation(
                                    x2h[:, :],
                                    xsrc_half(s, half),
                                    AF.Square).then_inc(S["aX2"], 1)

                        def _lnr(s):
                            a.wait_ge(S["dVAR"], s + 1)
                            ins = a.activation(lnv_r[:1, :], var_r[:1, :], AF.Ln,
                                               bias=eps_ln[:1, :])
                            fence(en, a, ins)
                            if s >= 1:
                                a.wait_ge(S["pBC"], 2 * s)
                                a.wait_ge(S["dMR"], s)
                            a.activation(rstd_r[:1, :], lnv_r[:1, :], AF.Exp,
                                         scale=-0.5).then_inc(S["aRSTD"], 1)
                            a.wait_ge(S["pBC"], 2 * s + 1)
                            if s >= 1:
                                a.wait_ge(S["dAPP"], s)
                            a.activation(rstd_bc[:, :], pC[:, :], AF.Identity
                                         ).then_inc(S["aBCE"], 1)
                            a.wait_ge(S["pBC"], 2 * s + 2)
                            if s >= 1:
                                a.wait_ge(S["dAPP"], s)
                            a.activation(mrstd_bc[:, :], pC[:, :], AF.Identity
                                         ).then_inc(S["aBCE"], 1)

                        def _kv(s):
                            for kv in range(KVH):
                                idx = cnt_pK(s, kv)
                                slot = kslot(kv)
                                a.wait_ge(S["pK"], idx)
                                if idx > 1:
                                    a.wait_ge(S["pKMS"], idx - 1)
                                a.activation(k2_bf[:, :], slot[:, :], AF.Square
                                             ).then_inc(S["aK2"], 1)
                                a.wait_ge(S["pKMS"], idx)
                                if idx > 1:
                                    a.wait_ge(S["pKBC"], idx - 1)
                                ins = a.activation(klnr[:1, :], pG[:1, :], AF.Ln,
                                                   scale=1.0 / D, bias=eps_rms[:1, :])
                                fence(en, a, ins)
                                a.activation(krstd_r[:1, :], klnr[:1, :], AF.Exp,
                                             scale=-0.5).then_inc(S["aKR"], 1)
                                a.wait_ge(S["pKBC"], idx)
                                if idx > 1:
                                    a.wait_ge(S["dK"], idx - 1)
                                a.activation(krstd_bc[:, :], pD[:, :], AF.Identity
                                             ).then_inc(S["aKBE"], 1)
                            for j in range(SEGC):
                                jdx = cnt_pV(s, j)
                                slot = vslot(s, j)
                                kpos = s * SEGC + j
                                a.wait_ge(S["pV"], jdx)
                                last[0] = a.activation(
                                    vT_own[:, kpos * KVD:(kpos + 1) * KVD],
                                    slot[:, :KVD], AF.Identity)
                                last[0].then_inc(S["aV"], 1)
                                if is_own(s):
                                    i = own_idx(s, j)
                                    if i > 0:
                                        a.wait_ge(S["dSS"], i)
                                    a.wait_ge(S["aV"], jdx)
                                    a.activation(v2_bf[:, :],
                                                 vT_own[:, kpos * KVD:(kpos + 1) * KVD],
                                                 AF.Square).then_inc(S["aV2"], 1)
                                    a.wait_ge(S["dSS"], i + 1)
                                    if i > 0:
                                        a.wait_ge(S["aVS"], i * KVH)
                                    ins = a.activation(sln_c[:, :], ssm_c[:, :], AF.Ln)
                                    fence(en, a, ins)
                                    a.activation(vrstd_c[:, :], sln_c[:, :], AF.Exp,
                                                 scale=-0.5).then_inc(S["aVR"], 1)
                                    a.wait_ge(S["aVR"], i + 1)
                                    if i > 0:
                                        a.wait_ge(S["pVT"], i * KVH)
                                    for kv in range(KVH):
                                        last[0] = a.activation(
                                            vn_buf[:, kv * D:(kv + 1) * D],
                                            vT_own[:, kpos * KVD + kv * D: kpos * KVD + (kv + 1) * D],
                                            AF.Identity, scale=vrstd_c[:, kv:kv + 1])
                                        last[0].then_inc(S["aVS"], 1)

                        for s in range(PSEGN):
                            _sq(s)
                            if s >= 1:
                                _kv(s - 1)
                            _lnr(s)
                        _kv(PSEGN - 1)
                        a.drain().then_inc(S["bar1"], 1)

                    @blk.vector
                    def _(v):
                        en = "vector"
                        last = [None]
                        for s in range(PSEGN):
                            v.wait_ge(S["pSTh"], 2 * s + 2)
                            if s >= 1:
                                v.wait_ge(S["pBC"], 2 * s)
                                v.wait_ge(S["aRSTD"], s)
                            ins = v.tensor_scalar(mean_r[:1, :], pA[:1, :], 1.0 / C,
                                                  None, ALU.mult)
                            fence(en, v, ins)
                            ins = v.tensor_scalar(ms_r[:1, :], pB[:1, :], 1.0 / C,
                                                  None, ALU.mult)
                            fence(en, v, ins)
                            ins = v.tensor_tensor(var_r[:1, :], mean_r[:1, :],
                                                  mean_r[:1, :], ALU.mult)
                            fence(en, v, ins)
                            ins = v.tensor_tensor(var_r[:1, :], ms_r[:1, :],
                                                  var_r[:1, :], ALU.subtract)
                            ins.then_inc(S["dVAR"], 1)
                            v.wait_ge(S["aRSTD"], s + 1)
                            v.tensor_tensor(mrstd_r[:1, :], mean_r[:1, :],
                                            rstd_r[:1, :], ALU.mult).then_inc(S["dMR"], 1)
                            xnt, xb, xstr = xnT_src(s)
                            v.wait_ge(S["aBCE"], 2 * s + 2)
                            for cc in range(DVE_CC):
                                ins = v.tensor_tensor(
                                    tmp_bf[:, :], xsrc(s, cc),
                                    rstd_bc[:, :], ALU.mult)
                                fence(en, v, ins)
                                ins = v.tensor_tensor(
                                    xnt[:, xb + cc * xstr: xb + cc * xstr + SEG],
                                    tmp_bf[:, :], mrstd_bc[:, :], ALU.subtract)
                                if cc < DVE_CC - 1:
                                    fence(en, v, ins)
                            ins.then_inc(S["dAPP"], 1)
                            for kv in range(KVH):
                                idx = cnt_pK(s, kv)
                                slot = kslot(kv)
                                v.wait_ge(S["aKBE"], idx)
                                v.tensor_tensor(
                                    kT_own[:, kv * QL + s * SEG: kv * QL + (s + 1) * SEG],
                                    slot[:, :], krstd_bc[:, :], ALU.mult
                                ).then_inc(S["dK"], 1)
                            if is_own(s):
                                for j in range(SEGC):
                                    i = own_idx(s, j)
                                    v.wait_ge(S["aV2"], i + 1)
                                    if i > 0:
                                        v.wait_ge(S["aVR"], i)
                                    ins = v.tensor_reduce(
                                        ss_c[:, :],
                                        v2_bf[:, :].rearrange("p (kv d) -> p kv d", d=D),
                                        AX.X, ALU.add)
                                    fence(en, v, ins)
                                    v.tensor_scalar(ssm_c[:, :], ss_c[:, :], VN_EPS2,
                                                    None, ALU.max).then_inc(S["dSS"], 1)
                                    for kv in range(KVH):
                                        ii = i * KVH + kv
                                        v.wait_ge(S["pVT"], ii + 1)
                                        tq = (s - own_s0) * SEG + j * D
                                        last[0] = v.tensor_copy(
                                            vnT[:, kv * QL + tq: kv * QL + tq + D],
                                            pTR[:, :])
                                        last[0].then_inc(S["dVN"], 1)
                        v.drain().then_inc(S["bar1"], 1)

            esA = ExitStack()
            with esA:
                sbA = lambda nm, shp, dt: esA.enter_context(nc.sbuf_tensor(nm, shp, dt))
                yT = sbA("yT", [128, H * QL], BF16)
                esQT = ExitStack()
                esQT.__enter__()
                qT = esQT.enter_context(nc.sbuf_tensor("qT", [128, H * QL], BF16))

                esQ = ExitStack()
                with esQ:
                    wq_sl = [esQ.enter_context(nc.sbuf_tensor(f"wq{i}", [128, CC * OSW], BF16))
                             for i in range(2)]
                    q2_bf = esQ.enter_context(nc.sbuf_tensor("q2_bf", [128, QL], BF16))
                    qln_r = esQ.enter_context(nc.sbuf_tensor("qln_r", [1, QL], F32))
                    qrstd_r = esQ.enter_context(nc.sbuf_tensor("qrstd_r", [1, QL], F32))
                    qbc_bf = esQ.enter_context(nc.sbuf_tensor("qbc_bf", [128, QL], BF16))
                    qp = [esQ.enter_context(nc.psum_tensor(f"qp{i}", [128, QL], F32))
                          for i in range(2)]
                    qms = esQ.enter_context(nc.psum_tensor("qms", [1, QL], F32))
                    qbc = esQ.enter_context(nc.psum_tensor("qbc", [128, QL], F32))

                    with Block_() as blk:
                        @blk.gpsimd
                        def _(g):
                            g.wait_ge(S["bar1"], 4)
                            g.wait_ge(S["dGK"], 16)
                            g.wait_ge(S["dGV"], 16)
                            for si in range(NOS):
                                if si >= 2:
                                    g.wait_ge(S["pQ"], (si - 1) * HPS)
                                g.dma_start(
                                    wq_sl[si % 2][:, :].rearrange("p (cc o) -> p cc o", o=OSW),
                                    wq[:, si * OSW:(si + 1) * OSW].rearrange(
                                        "(cc p) o -> p cc o", p=128),
                                ).then_inc(S["dWQ0" if si % 2 == 0 else "dWQ1"], 16)
                            g.wait_ge(S["dRK"], 16 * NR)
                            g.wait_ge(S["dRV"], 16 * NR)
                            g.sem_inc(S["bar1g"], 1)

                        @blk.tensor
                        def _(t):
                            t.wait_ge(S["bar1"], 4)
                            for si in range(NOS):
                                for hh in range(HPS):
                                    h = si * HPS + hh
                                    if h == 0:
                                        t.wait_ge(S["dWQH"], 16)
                                    if h == max(si * HPS, 1):
                                        t.wait_ge(S["dWQ0" if si % 2 == 0 else "dWQ1"],
                                                  16 * (si // 2 + 1))
                                    slot = qp[h % 2]
                                    if h >= 2:
                                        t.wait_ge(S["dQ"], h - 1)
                                    for cc in range(CC):
                                        wsrc = (wqh[:, cc * D:(cc + 1) * D] if h == 0 else
                                                wq_sl[si % 2][:, cc * OSW + hh * D: cc * OSW + (hh + 1) * D])
                                        for (qo, qw) in QH:
                                            ins = t.matmul(
                                                slot[:, qo:qo + qw],
                                                wsrc,
                                                xnT[:, cc * QL + qo: cc * QL + qo + qw],
                                                start=(cc == 0), stop=(cc == CC - 1))
                                    ins.then_inc(S["pQ"], 1)
                                    t.wait_ge(S["aQ2"], h + 1)
                                    if h >= 1:
                                        t.wait_ge(S["aQR"], h)
                                    for (qo, qw) in QH:
                                        ins = t.matmul(qms[:1, qo:qo + qw], ones_bf[:, :],
                                                       q2_bf[:, qo:qo + qw],
                                                       start=True, stop=True)
                                    ins.then_inc(S["pQMS"], 1)
                                    t.wait_ge(S["aQR"], h + 1)
                                    if h >= 1:
                                        t.wait_ge(S["aQBE"], h)
                                    for (qo, qw) in QH:
                                        ins = t.matmul(qbc[:, qo:qo + qw], onesr_f[:1, :],
                                                       qrstd_r[:1, qo:qo + qw],
                                                       start=True, stop=True)
                                    ins.then_inc(S["pQBC"], 1)
                            t.nop().then_inc(S["bar2"], 1)

                        @blk.scalar
                        def _(a):
                            en = "scalar"
                            a.wait_ge(S["bar1"], 4)
                            a.wait_ge(S["dGK"], 16)
                            a.wait_ge(S["dGV"], 16)
                            for h in range(H):
                                slot = qp[h % 2]
                                a.wait_ge(S["pQ"], h + 1)
                                if h >= 1:
                                    a.wait_ge(S["pQMS"], h)
                                a.activation(q2_bf[:, :], slot[:, :], AF.Square
                                             ).then_inc(S["aQ2"], 1)
                                a.wait_ge(S["pQMS"], h + 1)
                                if h >= 1:
                                    a.wait_ge(S["pQBC"], h)
                                ins = a.activation(qln_r[:1, :], qms[:1, :], AF.Ln,
                                                   scale=1.0 / D, bias=eps_rms[:1, :])
                                fence(en, a, ins)
                                a.activation(qrstd_r[:1, :], qln_r[:1, :], AF.Exp,
                                             scale=-0.5).then_inc(S["aQR"], 1)
                                a.wait_ge(S["pQBC"], h + 1)
                                if h >= 1:
                                    a.wait_ge(S["dQ"], h)
                                ins = a.activation(qbc_bf[:, :], qbc[:, :], AF.Identity)
                                ins.then_inc(S["aQBE"], 1)
                            a.drain().then_inc(S["bar2"], 1)

                        @blk.vector
                        def _(v):
                            v.wait_ge(S["bar1"], 4)
                            v.wait_ge(S["dGK"], 16)
                            v.wait_ge(S["dGV"], 16)
                            for h in range(H):
                                slot = qp[h % 2]
                                v.wait_ge(S["aQBE"], h + 1)
                                ins = v.tensor_tensor(qT[:, h * QL:(h + 1) * QL],
                                                      slot[:, :], qbc_bf[:, :], ALU.mult)
                                ins.then_inc(S["dQ"], 1)
                            v.drain().then_inc(S["bar2"], 1)

                esAA = ExitStack()
                with esAA:
                    sbAA = lambda nm, shp, dt: esAA.enter_context(nc.sbuf_tensor(nm, shp, dt))
                    pT = [sbAA(f"pT{i}", [128, QL], BF16) for i in range(8)]
                    acc_d = sbAA("acc_d", [128, QL], F32)
                    if CC * D >= 2 * QL:
                        tmp_d = [wqh[:, 0:QL], wqh[:, QL:2 * QL]]
                    else:
                        tmp_d = [sbAA(f"tmp_d{i}", [128, QL], BF16)[:, :] for i in range(2)]
                    t_bf = sbAA("t_bf", [128, QL], BF16)
                    u_f = sbAA("u_f", [128, QL], F32)
                    r_r = sbAA("r_r", [1, QL], F32)
                    dr_r = sbAA("dr_r", [1, QL], BF16)
                    Sp = [esAA.enter_context(nc.psum_tensor(f"Sp{i}", [128, QL], F32))
                          for i in range(2)]
                    y_ps = esAA.enter_context(nc.psum_tensor("y_ps", [128, QL], F32))
                    TL = esAA.enter_context(nc.psum_tensor("TL", [128, QL], F32))
                    NPAIR = KCH // 2
                    USE_SLOTS = KCH >= 18 and (KCH // 2) >= 11
                    TSLOT = {5: 0, 9: 1, 13: 2, 17: 3} if USE_SLOTS else {}

                    with Block_() as blk:
                        @blk.tensor
                        def _(t):
                            t.wait_ge(S["bar2"], 3)
                            t.wait_ge(S["bar1g"], 1)

                            def tail_mm(t, hp, stage):
                                kv = hp // REP
                                if stage == 0:
                                    t.wait_ge(S["dA"], (hp + 1) * (KCH // 2))
                                    if hp >= 1:
                                        t.wait_ge(S["aR2"], 2 * hp)
                                    for (qo, qw) in QH:
                                        ins = t.matmul(TL[0:1, qo:qo + qw], ones_f[:, :],
                                                       acc_d[:, qo:qo + qw],
                                                       start=True, stop=True)
                                    ins.then_inc(S["pROW"], 1)
                                elif stage == 1:
                                    t.wait_ge(S["dT"], hp + 1)
                                    t.wait_ge(S["aR1"], hp + 1)
                                    for (qo, qw) in QH:
                                        ins = t.matmul(TL[0:1, qo:qo + qw], ones_bf[:, :],
                                                       t_bf[:, qo:qo + qw],
                                                       start=True, stop=True)
                                    ins.then_inc(S["pROW"], 1)
                                elif stage == 2:
                                    t.wait_ge(S["aR1"], hp + 1)
                                    t.wait_ge(S["dROWd"], hp + 1)
                                    for (qo, qw) in QH:
                                        ins = t.matmul(TL[:, qo:qo + qw], onesr_f[:1, :],
                                                       r_r[:1, qo:qo + qw],
                                                       start=True, stop=True)
                                    ins.then_inc(S["pBC2"], 1)
                                else:
                                    t.wait_ge(S["dROWd"], hp + 1)
                                    t.wait_ge(S["aR2"], 2 * hp + 1)
                                    for (qo, qw) in QH:
                                        ins = t.matmul(TL[:, qo:qo + qw], onesr_bf[:1, :],
                                                       dr_r[:1, qo:qo + qw],
                                                       start=True, stop=True)
                                    ins.then_inc(S["pBC2"], 1)

                            def emit_pv(t, h, j):
                                kv = h // REP
                                idx = h * KCH + j
                                t.wait_ge(S["aE"], idx + 1)
                                if j == 0 and h >= 1:
                                    t.wait_ge(S["dYE"], h)
                                for (qo, qw) in QH:
                                    ins = t.matmul(
                                        y_ps[:, qo:qo + qw],
                                        vT[:, j * KVD + kv * D: j * KVD + (kv + 1) * D],
                                        pT[idx % 8][:, qo:qo + qw],
                                        start=(j == 0), stop=(j == KCH - 1))
                                ins.then_inc(S["pPV"], 1)

                            for h in range(H):
                                kv = h // REP
                                if h >= 1 and not TSLOT:
                                    for st in range(4):
                                        tail_mm(t, h - 1, st)
                                for j in range(KCH):
                                    idx = h * KCH + j
                                    if idx >= 2:
                                        t.wait_ge(S["aE"], idx - 1)
                                    for (qo, qw) in QH:
                                        ins = t.matmul(
                                            Sp[j % 2][:, qo:qo + qw],
                                            kT[:, kv * NN + j * D: kv * NN + (j + 1) * D],
                                            qT[:, h * QL + qo: h * QL + qo + qw],
                                            start=True, stop=True)
                                    ins.then_inc(S["pS"], 1)
                                    if h >= 1 and j in TSLOT:
                                        tail_mm(t, h - 1, TSLOT[j])
                                    if j == 2:
                                        emit_pv(t, h, 0)
                                        emit_pv(t, h, 1)
                                    elif j >= 3:
                                        emit_pv(t, h, j - 1)
                                emit_pv(t, h, KCH - 1)
                            for st in range(4):
                                tail_mm(t, H - 1, st)
                            t.nop().then_inc(S["bar3"], 1)

                        @blk.gpsimd
                        def _(g):
                            for h in range(H):
                                for p in range(NPAIR):
                                    pg = h * NPAIR + p
                                    g.wait_ge(S["dP"], pg + 1)
                                    if pg >= 1:
                                        g.wait_ge(S["dA"], pg)
                                    if p == 0:
                                        if h >= 1:
                                            g.wait_ge(S["pROW"], 2 * h - 1)
                                        ins = g.tensor_copy(acc_d[:, :], tmp_d[pg % 2])
                                    else:
                                        ins = g.tensor_tensor(acc_d[:, :], acc_d[:, :],
                                                              tmp_d[pg % 2], ALU.add)
                                    ins.then_inc(S["dA"], 1)
                            g.drain().then_inc(S["bar3"], 1)

                        @blk.scalar
                        def _(a):
                            en = "scalar"
                            a.wait_ge(S["bar2"], 3)
                            a.wait_ge(S["bar1g"], 1)

                            def tail_act(a, hp, stage):
                                a.wait_ge(S["pROW"], 2 * hp + 1)
                                if hp >= 1:
                                    a.wait_ge(S["pBC2"], 2 * hp - 1)
                                    a.wait_ge(S["dROWd"], hp)
                                ins = a.activation(r_r[:1, :], TL[0:1, :], AF.Ln)
                                fence(en, a, ins)
                                a.activation(r_r[:1, :], r_r[:1, :], AF.Exp,
                                             scale=-1.0).then_inc(S["aR1"], 1)

                            ASLOT = {7: 0} if USE_SLOTS else {}
                            for h in range(H):
                                if h >= 1 and not ASLOT:
                                    tail_act(a, h - 1, 0)
                                for j in range(KCH):
                                    idx = h * KCH + j
                                    a.wait_ge(S["pS"], idx + 1)
                                    if idx >= 8:
                                        a.wait_ge(S["pPV"], idx - 7)
                                        a.wait_ge(S["dP"], (idx - 8) // 2 + 1)
                                    a.activation(pT[idx % 8][:, :], Sp[j % 2][:, :],
                                                 AF.Exp, scale=SCALE).then_inc(S["aE"], 1)
                                    if h >= 1 and j in ASLOT:
                                        tail_act(a, h - 1, ASLOT[j])
                                a.wait_ge(S["pPV"], (h + 1) * KCH)
                                if h >= 1:
                                    a.wait_ge(S["dY"], h)
                                a.activation(u_f[:, :], y_ps[:, :], AF.Identity
                                             ).then_inc(S["dYE"], 1)
                            tail_act(a, H - 1, 0)
                            a.drain().then_inc(S["bar3"], 1)

                        @blk.vector
                        def _(v):
                            en = "vector"
                            v.wait_ge(S["bar2"], 3)
                            v.wait_ge(S["bar1g"], 1)

                            def tail_dve(v, hp, stage):
                                kv = hp // REP
                                if stage == 0:
                                    v.wait_ge(S["pROW"], 2 * hp + 2)
                                    v.wait_ge(S["aR1"], hp + 1)
                                    if hp >= 1:
                                        v.wait_ge(S["pBC2"], 2 * hp)
                                    ins = v.tensor_tensor(dr_r[:1, :], TL[0:1, :],
                                                          r_r[:1, :], ALU.mult)
                                    ins.then_inc(S["dROWd"], 1)
                                elif stage == 1:
                                    v.wait_ge(S["pBC2"], 2 * hp + 1)
                                    ins = v.tensor_tensor(u_f[:, :], u_f[:, :],
                                                          TL[:, :], ALU.mult)
                                    ins.then_inc(S["aR2"], 1)
                                elif stage == 2:
                                    v.wait_ge(S["pBC2"], 2 * hp + 2)
                                    v.wait_ge(S["pROW"], 2 * hp + 2)
                                    ins = v.tensor_tensor(t_bf[:, :], TL[:, :],
                                                          vnT[:, kv * QL:(kv + 1) * QL],
                                                          ALU.mult)
                                    ins.then_inc(S["aR2"], 1)
                                else:
                                    v.wait_ge(S["aR2"], 2 * hp + 2)
                                    ins = v.tensor_tensor(yT[:, hp * QL:(hp + 1) * QL],
                                                          u_f[:, :], t_bf[:, :],
                                                          ALU.subtract)
                                    ins.then_inc(S["dY"], 1)

                            VSLOT = {4: 0, 7: 1, 9: 2, 10: 3} if USE_SLOTS else {}
                            NPAIR = KCH // 2
                            for h in range(H):
                                kv = h // REP
                                if h >= 1 and not VSLOT:
                                    for st in range(4):
                                        tail_dve(v, h - 1, st)
                                for p in range(NPAIR):
                                    j0 = 2 * p
                                    idx1 = h * KCH + j0 + 1
                                    pg = h * NPAIR + p
                                    v.wait_ge(S["aE"], idx1 + 1)
                                    if pg >= 2:
                                        v.wait_ge(S["dA"], pg - 1)
                                    g_idx = h * KCH + j0
                                    ins = v.tensor_tensor(tmp_d[pg % 2],
                                                          pT[(g_idx % 8)][:, :],
                                                          pT[(g_idx % 8) + 1][:, :], ALU.add)
                                    ins.then_inc(S["dP"], 1)
                                    if h >= 1 and p in VSLOT:
                                        tail_dve(v, h - 1, VSLOT[p])
                                v.wait_ge(S["dYE"], h + 1)
                                if h >= 1:
                                    v.wait_ge(S["pROW"], 2 * h)
                                    v.wait_ge(S["dY"], h)
                                ins = v.tensor_tensor(t_bf[:, :], u_f[:, :],
                                                      vnT[:, kv * QL:(kv + 1) * QL], ALU.mult)
                                ins.then_inc(S["dT"], 1)
                            for st in range(4):
                                tail_dve(v, H - 1, st)
                            v.drain().then_inc(S["bar3"], 1)

                esQT.__exit__(None, None, None)

                esW = ExitStack()
                with esW:
                    wo_sl = [esW.enter_context(nc.sbuf_tensor(f"wo{i}", [128, H * COLW], BF16))
                             for i in range(2)]
                    xr_b = [esW.enter_context(nc.sbuf_tensor(f"xr{i}", [128, COLW], F32))
                            for i in range(4)]
                    sc_b = [esW.enter_context(nc.sbuf_tensor(f"sc{i}", [128, COLW], F32))
                            for i in range(2)]
                    wps = [esW.enter_context(nc.psum_tensor(f"wp{i}", [128, COLW], F32))
                           for i in range(QT)]

                    with Block_() as blk:
                        @blk.gpsimd
                        def _(g):
                            g.wait_ge(S["pS"], H * KCH)
                            for cq in range(NCQ):
                                if cq >= 2:
                                    g.wait_ge(S["pW"], (cq - 1) * QT)
                                g.dma_start(
                                    wo_sl[cq % 2][:, :].rearrange("p (dc o) -> p dc o", o=COLW),
                                    wo[:, cq * COLW:(cq + 1) * COLW].rearrange(
                                        "(dc p) o -> p dc o", p=128),
                                ).then_inc(S["dWO0" if cq % 2 == 0 else "dWO1"], 16)

                        @blk.sync
                        def _(sy):
                            for cq in range(NCQ):
                                for qt in range(QT):
                                    i = cq * QT + qt
                                    if i < 4:
                                        sy.wait_ge(S["dY"], H)
                                    else:
                                        sy.wait_ge(S["dW"], i - 3)
                                    sy.dma_start(xr_b[i % 4][:, :],
                                                 xr[qt * D:(qt + 1) * D,
                                                    cq * COLW:(cq + 1) * COLW]
                                                 ).then_inc(S[f"dXR{i % 4}"], 16)

                        @blk.scalar
                        def _(a):
                            for cq in range(NCQ):
                                for qt in range(QT):
                                    i = cq * QT + qt
                                    a.wait_ge(S["dW"], i + 1)
                                    a.dma_start(scr1[qt * D:(qt + 1) * D,
                                                     cq * COLW:(cq + 1) * COLW],
                                                sc_b[i % 2][:, :]).then_inc(S["dSC0" if i % 2 == 0 else "dSC1"], 16)

                        @blk.tensor
                        def _(t):
                            t.wait_ge(S["dY"], H)
                            for cq in range(NCQ):
                                t.wait_ge(S["dWO0" if cq % 2 == 0 else "dWO1"], 16 * (cq // 2 + 1))
                                for qt in range(QT):
                                    if cq >= 1:
                                        t.wait_ge(S["dW"], (cq - 1) * QT + qt + 1)
                                    for dc in range(H):
                                        ins = t.matmul(
                                            wps[qt][:, :COLW],
                                            yT[:, dc * QL + qt * D: dc * QL + (qt + 1) * D],
                                            wo_sl[cq % 2][:, dc * COLW:(dc + 1) * COLW],
                                            start=(dc == 0), stop=(dc == H - 1))
                                    ins.then_inc(S["pW"], 1)

                        @blk.vector
                        def _(v):
                            v.wait_ge(S["dY"], H)
                            for cq in range(NCQ):
                                for qt in range(QT):
                                    i = cq * QT + qt
                                    v.wait_ge(S["pW"], i + 1)
                                    v.wait_ge(S[f"dXR{i % 4}"], 16 * (i // 4 + 1))
                                    if i >= 2:
                                        v.wait_ge(S["dSC0" if i % 2 == 0 else "dSC1"], 16 * (i // 2))
                                    ins = v.tensor_tensor(sc_b[i % 2][:, :],
                                                          wps[qt][:, :COLW],
                                                          xr_b[i % 4][:, :], ALU.add)
                                    ins.then_inc(S["dW"], 1)
                            v.drain().then_inc(S["bar4"], 1)

        esF = ExitStack()
        with esF:
            sbF = lambda nm, shp, dt: esF.enter_context(nc.sbuf_tensor(nm, shp, dt))
            gatedT = sbF("gatedT", [128, FFC * QL], BF16)
            silu_bf = sbF("silu_bf", [128, QL], BF16)
            wg_sl = [sbF(f"wg{i}", [128, CC * HSW], BF16) for i in range(2)]
            wu_sl = [sbF(f"wu{i}", [128, CC * HSW], BF16) for i in range(2)]
            wd_sl = [sbF(f"wd{i}", [128, GH * COLW], BF16) for i in range(2)]
            sr_b = [sbF(f"sr{i}", [128, COLW], F32) for i in range(4)]
            ou_b = [sbF(f"ou{i}", [128, COLW], F32) for i in range(2)]

            wd_done = set()

            def wd_dma(g, i):
                cq, hg = divmod(i, NG)
                g.dma_start(
                    wd_sl[i % 2][:, :].rearrange("p (hc o) -> p hc o", o=COLW),
                    wd[hg * GH * D:(hg + 1) * GH * D,
                       cq * COLW:(cq + 1) * COLW].rearrange(
                        "(hc p) o -> p hc o", p=128)
                ).then_inc(S["dWD0" if i % 2 == 0 else "dWD1"], 16)

            esG = ExitStack()
            with esG:
                g_ps = [esG.enter_context(nc.psum_tensor(f"g{i}", [128, QL], F32))
                        for i in range(2)]
                u_ps = [esG.enter_context(nc.psum_tensor(f"u{i}", [128, QL], F32))
                        for i in range(2)]

                with Block_() as blk:
                    @blk.gpsimd
                    def _(g):
                        NQ_ = NCQ * QT
                        for hs in range(NHS):
                            if hs == 0:
                                g.wait_ge(S["pW"], NQ_)
                            elif hs >= 2:
                                g.wait_ge(S["pGU"], (hs - 1) * HPG)
                            g.dma_start(
                                wg_sl[hs % 2][:, :].rearrange("p (cc o) -> p cc o", o=HSW),
                                wg[:, hs * HSW:(hs + 1) * HSW].rearrange(
                                    "(cc p) o -> p cc o", p=128)).then_inc(
                                S["dWG0" if hs % 2 == 0 else "dWG1"], 16)
                            if hs == 0:
                                g.wait_ge(S["dW"], NQ_)
                            g.dma_start(
                                wu_sl[hs % 2][:, :].rearrange("p (cc o) -> p cc o", o=HSW),
                                wu[:, hs * HSW:(hs + 1) * HSW].rearrange(
                                    "(cc p) o -> p cc o", p=128)).then_inc(
                                S["dWG0" if hs % 2 == 0 else "dWG1"], 16)
                            if hs == 1 and NCQ * NG >= 1:
                                g.wait_ge(S["dSC0"], 16 * ((NQ_ + 1) // 2))
                                g.wait_ge(S["dSC1"], 16 * (NQ_ // 2))
                                wd_done.add(0)
                                wd_dma(g, 0)
                            elif hs == 3 and NCQ * NG >= 2:
                                wd_done.add(1)
                                wd_dma(g, 1)

                    @blk.tensor
                    def _(t):
                        t.wait_ge(S["bar4"], 1)
                        for hs in range(NHS):
                            t.wait_ge(S["dWG0" if hs % 2 == 0 else "dWG1"], 32 * (hs // 2 + 1))
                            for hh in range(HPG):
                                hc = hs * HPG + hh
                                p = hc % 2
                                if hc >= 2:
                                    t.wait_ge(S["dGT"], hc - 1)
                                for cc in range(CC):
                                    for (qo, qw) in QH:
                                        t.matmul(
                                            g_ps[p][:, qo:qo + qw],
                                            wg_sl[hs % 2][:, cc * HSW + hh * D: cc * HSW + (hh + 1) * D],
                                            xnT[:, cc * QL + qo: cc * QL + qo + qw],
                                            start=(cc == 0), stop=(cc == CC - 1))
                                for cc in range(CC):
                                    for (qo, qw) in QH:
                                        ins = t.matmul(
                                            u_ps[p][:, qo:qo + qw],
                                            wu_sl[hs % 2][:, cc * HSW + hh * D: cc * HSW + (hh + 1) * D],
                                            xnT[:, cc * QL + qo: cc * QL + qo + qw],
                                            start=(cc == 0), stop=(cc == CC - 1))
                                ins.then_inc(S["pGU"], 1)

                    @blk.scalar
                    def _(a):
                        for hc in range(FFC):
                            p = hc % 2
                            a.wait_ge(S["pGU"], hc + 1)
                            if hc >= 1:
                                a.wait_ge(S["dGT"], hc)
                            ins = a.activation(silu_bf[:, :], g_ps[p][:, :], AF.Silu)
                            ins.then_inc(S["aSIL"], 1)

                    @blk.vector
                    def _(v):
                        for hc in range(FFC):
                            p = hc % 2
                            v.wait_ge(S["aSIL"], hc + 1)
                            ins = v.tensor_tensor(gatedT[:, hc * QL:(hc + 1) * QL],
                                                  u_ps[p][:, :], silu_bf[:, :], ALU.mult)
                            ins.then_inc(S["dGT"], 1)

            esD = ExitStack()
            with esD:
                dps = [esD.enter_context(nc.psum_tensor(f"d{i}", [128, COLW], F32))
                       for i in range(QT)]

                with Block_() as blk:
                    @blk.gpsimd
                    def _(g):
                        NQ_ = NCQ * QT
                        for i in (0, 1):
                            if i not in wd_done and i < NCQ * NG:
                                g.wait_ge(S["dSC0"], 16 * ((NQ_ + 1) // 2))
                                g.wait_ge(S["dSC1"], 16 * (NQ_ // 2))
                                wd_dma(g, i)
                        for i in range(2, NCQ * NG):
                            cq2, hg2 = divmod(i - 2, NG)
                            if hg2 == NG - 1:
                                g.wait_ge(S["pDN"], cq2 * QT + QT)
                            else:
                                g.wait_ge(S["fWD"], sum(
                                    1 for k in range(i - 1) if k % NG != NG - 1))
                            wd_dma(g, i)

                    @blk.sync
                    def _(sy):
                        NQ_ = NCQ * QT
                        sy.wait_ge(S["dSC0"], 16 * ((NQ_ + 1) // 2))
                        sy.wait_ge(S["dSC1"], 16 * (NQ_ // 2))
                        for cq in range(NCQ):
                            for qt in range(QT):
                                i = cq * QT + qt
                                if i >= 4:
                                    sy.wait_ge(S["dF"], i - 3)
                                sy.dma_start(sr_b[i % 4][:, :],
                                             scr1[qt * D:(qt + 1) * D,
                                                  cq * COLW:(cq + 1) * COLW]
                                             ).then_inc(S[f"dSC2{i % 4}"], 16)

                    @blk.scalar
                    def _(a):
                        NQ_ = NCQ * QT
                        for cq in range(NCQ):
                            for qt in range(QT):
                                i = cq * QT + qt
                                a.wait_ge(S["dF"], i + 1)
                                a.dma_start(out[qt * D:(qt + 1) * D,
                                                cq * COLW:(cq + 1) * COLW],
                                            ou_b[i % 2][:, :]).then_inc(
                                    S["dOUT0" if i % 2 == 0 else "dOUT1"], 16)
                        a.wait_ge(S["dOUT0"], 16 * ((NQ_ + 1) // 2))
                        a.wait_ge(S["dOUT1"], 16 * (NQ_ // 2))

                    @blk.tensor
                    def _(t):
                        t.wait_ge(S["dGT"], FFC)
                        t.wait_ge(S["aSIL"], FFC)
                        for cq in range(NCQ):
                            for hg in range(NG):
                                i = cq * NG + hg
                                t.wait_ge(S["dWD0" if i % 2 == 0 else "dWD1"], 16 * (i // 2 + 1))
                                if hg != NG - 1:
                                    for h16 in range(GH):
                                        h = hg * GH + h16
                                        for qt in range(QT):
                                            if h == 0 and cq >= 1:
                                                t.wait_ge(S["dF"], (cq - 1) * QT + qt + 1)
                                            ins = t.matmul(
                                                dps[qt][:, :COLW],
                                                gatedT[:, h * QL + qt * D: h * QL + (qt + 1) * D],
                                                wd_sl[i % 2][:, h16 * COLW:(h16 + 1) * COLW],
                                                start=(h == 0), stop=False)
                                    ins.then_inc(S["fWD"], 1)
                                else:
                                    for qt in range(QT):
                                        for h16 in range(GH):
                                            h = hg * GH + h16
                                            if h == 0 and cq >= 1:
                                                t.wait_ge(S["dF"], (cq - 1) * QT + qt + 1)
                                            ins = t.matmul(
                                                dps[qt][:, :COLW],
                                                gatedT[:, h * QL + qt * D: h * QL + (qt + 1) * D],
                                                wd_sl[i % 2][:, h16 * COLW:(h16 + 1) * COLW],
                                                start=(h == 0), stop=(h == FFC - 1))
                                            if h == FFC - 1:
                                                ins.then_inc(S["pDN"], 1)

                    @blk.vector
                    def _(v):
                        for cq in range(NCQ):
                            for qt in range(QT):
                                i = cq * QT + qt
                                v.wait_ge(S["pDN"], i + 1)
                                v.wait_ge(S[f"dSC2{i % 4}"], 16 * (i // 4 + 1))
                                if i >= 2:
                                    v.wait_ge(S["dOUT0" if i % 2 == 0 else "dOUT1"], 16 * (i // 2))
                                ins = v.tensor_tensor(ou_b[i % 2][:, :],
                                                      dps[qt][:, :COLW],
                                                      sr_b[i % 4][:, :], ALU.add)
                                ins.then_inc(S["dF"], 1)

    return nc






from concourse.bass_utils import run_bass_kernel_spmd

_C, _NN, _QL, _FF, _H, _KVH, _SEG = 2048, 4096, 1024, 8192, 16, 4, 512
_NB, _NCORE = 2, 8
_CPB = _NCORE // _NB


def kernel(L0_attn_x, L0_attn_Wq, L0_attn_Wk, L0_attn_Wv, L0_attn_Wo,
           L0_ffn_w_gate, L0_ffn_w_up, L0_ffn_w_down):
    x = np.asarray(L0_attn_x, dtype=np.float32)
    w = {
        "wq": np.ascontiguousarray(np.asarray(L0_attn_Wq, np.float32).T),
        "wk": np.ascontiguousarray(np.asarray(L0_attn_Wk, np.float32).T),
        "wv": np.ascontiguousarray(np.asarray(L0_attn_Wv, np.float32).T),
        "wo": np.ascontiguousarray(np.asarray(L0_attn_Wo, np.float32).T),
        "wg": np.ascontiguousarray(np.asarray(L0_ffn_w_gate, np.float32).T),
        "wu": np.ascontiguousarray(np.asarray(L0_ffn_w_up, np.float32).T),
        "wd": np.ascontiguousarray(np.asarray(L0_ffn_w_down, np.float32).T),
    }
    in_maps = []
    for c in range(_NCORE):
        b, r = divmod(c, _CPB)
        q0 = r * _QL
        m = {"xT": np.ascontiguousarray(x[b, q0:q0 + _QL].T),
             "xr": np.ascontiguousarray(x[b, q0:q0 + _QL])}
        m.update(w)
        in_maps.append(m)
    nc = build_core(_C, _NN, _QL, 0, _FF, _H, _KVH, _SEG,
                    NR=_CPB, NCORES=_NCORE)
    res = run_bass_kernel_spmd(nc, in_maps, core_ids=list(range(_NCORE)))
    out = np.empty((_NB, _NN, _C), dtype=np.float32)
    for c in range(_NCORE):
        b, r = divmod(c, _CPB)
        out[b, r * _QL:(r + 1) * _QL, :] = res.results[c]["out"]
    return out


# version 37
# speedup vs baseline: 1.0622x; 1.0107x over previous
from contextlib import ExitStack
import numpy as np
import concourse.bass as bass
import concourse.mybir as mybir

F32 = mybir.dt.float32
BF16 = mybir.dt.bfloat16
AF = mybir.ActivationFunctionType
ALU = mybir.AluOpType
AX = mybir.AxisListType

D = 128
LN_EPS = 1e-5
RMS_EPS = float(np.finfo(np.float32).eps)
VN_EPS2 = 1e-24


def halves(total, w=512):
    return [(o, min(w, total - o)) for o in range(0, total, w)]


def build_core(C, NN, QL, q0, FF, H, KVH, SEG=512, GHmax=8, NR=1, NCORES=1):
    CC = C // D
    SEGN = NN // SEG
    SEGC = SEG // D
    KCH = NN // D
    QT = QL // D
    FFC = FF // D
    KVD = KVH * D
    REP = H // KVH
    QH = halves(QL)
    OSW = min(256, H * D)
    NOS = (H * D) // OSW
    HPS = OSW // D
    HSW = min(128, FF)
    NHS = FF // HSW
    HPG = HSW // D
    GH = min(GHmax, FFC)
    NG = FFC // GH
    CH = halves(C)
    COLW = CH[0][1]
    NCQ = len(CH)
    own_s0 = 0
    own_sn = QL // SEG
    PSEGN = QL // SEG
    assert q0 == 0 and NN == NR * QL
    SCALE = float(1.0 / np.sqrt(D))
    assert KVH % 2 == 0 and SEGC % 2 == 0 and QT <= 8

    nc = bass.Bass("TRN2", target_bir_lowering=False)
    xT = nc.dram_tensor("xT", [C, QL], F32, kind="ExternalInput").ap()
    xr = nc.dram_tensor("xr", [QL, C], F32, kind="ExternalInput").ap()
    wq = nc.dram_tensor("wq", [C, H * D], F32, kind="ExternalInput").ap()
    wk = nc.dram_tensor("wk", [C, KVD], F32, kind="ExternalInput").ap()
    wv = nc.dram_tensor("wv", [C, KVD], F32, kind="ExternalInput").ap()
    wo = nc.dram_tensor("wo", [H * D, C], F32, kind="ExternalInput").ap()
    wg = nc.dram_tensor("wg", [C, FF], F32, kind="ExternalInput").ap()
    wu = nc.dram_tensor("wu", [C, FF], F32, kind="ExternalInput").ap()
    wd = nc.dram_tensor("wd", [FF, C], F32, kind="ExternalInput").ap()
    out = nc.dram_tensor("out", [QL, C], F32, kind="ExternalOutput").ap()
    scr1 = nc.dram_tensor("scr1", [QL, C], F32, kind="Internal").ap()
    st_k = nc.dram_tensor("st_k", [KVH * D, QL], BF16, kind="Internal").ap()
    st_v = nc.dram_tensor("st_v", [(QL // D) * D, KVD], BF16, kind="Internal").ap()
    G_k = nc.dram_tensor("G_k", [NR * KVH * D, QL], BF16, kind="Internal").ap()
    G_v = nc.dram_tensor("G_v", [NR * QL, KVD], BF16, kind="Internal").ap()

    es = ExitStack()
    with es:
        sb = lambda nm, shp, dt: es.enter_context(nc.sbuf_tensor(nm, shp, dt))
        sem = lambda nm: es.enter_context(nc.semaphore(nm))
        xnT = sb("xnT", [128, CC * QL], BF16)
        ones_bf = sb("ones_bf", [128, 1], BF16)
        ones_f = sb("ones_f", [128, 1], F32)
        onesr_f = sb("onesr_f", [1, 128], F32)
        onesr_bf = sb("onesr_bf", [1, 128], BF16)
        ident = sb("ident", [128, 128], BF16)
        eps_ln = sb("eps_ln", [1, 1], F32)
        eps_rms = sb("eps_rms", [1, 1], F32)

        names = ("gI dWKV dXA0 dXA1 aX2 pSTh dGK dGV ccK ccV dRK dRV dVAR aRSTD dMR pBC aBCE dAPP pAPP "
                 "pK aK2 pKMS aKR pKBC aKBE dK pV aV "
                 "aV2 dSS aVR aVS pVT dVN "
                 "dWQH dWQ0 dWQ1 fWQ pQ aQ2 pQMS aQR pQBC aQBE dQ "
                 "pS aE pPV dA dP gQ dT dYE pROW dROWd aR1 pBC2 aR2 dY "
                 "dWO0 dWO1 pW dXR0 dXR1 dXR2 dXR3 dW dSC0 dSC1 "
                 "dWG0 dWG1 fWG pGU aSIL dGT dWD0 dWD1 fWD pDN dSC20 dSC21 dSC22 dSC23 dF dOUT0 dOUT1 "
                 "bar1 bar1g bar2 bar3 bar4 "
                 "slfT slfS slfV slfG slfY").split()
        S = {n: sem(n) for n in names}
        slf_cnt = {"tensor": 0, "scalar": 0, "vector": 0, "gpsimd": 0, "sync": 0}
        slf_sem = {"tensor": S["slfT"], "scalar": S["slfS"], "vector": S["slfV"],
                   "gpsimd": S["slfG"], "sync": S["slfY"]}

        def fence(en, e, ins):
            slf_cnt[en] += 1
            ins.then_inc(slf_sem[en], 1)
            e.wait_ge(slf_sem[en], slf_cnt[en])

        import contextlib
        _outer_blk = es.enter_context(nc.Block())

        def Block_():
            return contextlib.nullcontext(_outer_blk)

        def cnt_pK(s, kv):
            return s * KVH + kv + 1

        def cnt_pV(s, j):
            return s * SEGC + j + 1

        def is_own(s):
            return own_s0 <= s < own_s0 + own_sn

        def own_idx(s, j):
            return (s - own_s0) * SEGC + j


        with Block_() as blk:
            @blk.gpsimd
            def _(g):
                g.memset(ones_bf[:, :], 1.0).then_inc(S["gI"], 1)
                g.memset(ones_f[:, :], 1.0).then_inc(S["gI"], 1)
                g.memset(onesr_f[:1, :], 1.0).then_inc(S["gI"], 1)
                g.memset(onesr_bf[:1, :], 1.0).then_inc(S["gI"], 1)
                g.memset(ident[:, :], 0.0).then_inc(S["gI"], 1)
                g.memset(eps_ln[:1, :], LN_EPS).then_inc(S["gI"], 1)
                g.memset(eps_rms[:1, :], RMS_EPS).then_inc(S["gI"], 1)
                g.wait_ge(S["gI"], 7)
                g.affine_select(out=ident[:, :], in_=ident[:, :],
                                compare_op=ALU.not_equal, fill=1.0, base=0,
                                pattern=[[-1, 128]], channel_multiplier=1
                                ).then_inc(S["gI"], 1)

        es2 = ExitStack()
        with es2:
            sb2 = lambda nm, shp, dt: es2.enter_context(nc.sbuf_tensor(nm, shp, dt))
            kT = sb2("kT", [128, KVH * NN], BF16)
            vT = sb2("vT", [128, KCH * KVD], BF16)
            vnT = sb2("vnT", [128, KVH * QL], BF16)
            wqh = sb2("wqh", [128, CC * D], BF16)

            esP = ExitStack()
            with esP:
                sbP = lambda nm, shp, dt: esP.enter_context(nc.sbuf_tensor(nm, shp, dt))
                psP = lambda nm, shp, dt=F32: esP.enter_context(nc.psum_tensor(nm, shp, dt))
                xA = [sbP("xA0", [128, CC * SEG], BF16)]
                x2h = sbP("x2h", [128, (CC // 2) * SEG], BF16)
                kT_own = sbP("kT_own", [128, KVH * QL], BF16)
                vT_own = sbP("vT_own", [128, (QL // D) * KVD], BF16)
                wk_bf = sbP("wk_bf", [128, CC * KVD], BF16)
                wv_bf = sbP("wv_bf", [128, CC * KVD], BF16)
                mean_r = sbP("mean_r", [1, SEG], F32)
                ms_r = sbP("ms_r", [1, SEG], F32)
                var_r = sbP("var_r", [1, SEG], F32)
                lnv_r = sbP("lnv_r", [1, SEG], F32)
                rstd_r = sbP("rstd_r", [1, SEG], F32)
                mrstd_r = sbP("mrstd_r", [1, SEG], F32)
                rstd_bc = sbP("rstd_bc", [128, SEG], BF16)
                mrstd_bc = sbP("mrstd_bc", [128, SEG], BF16)
                tmp_bf = sbP("tmp_bf", [128, SEG], BF16)
                k2_bf = sbP("k2_bf", [128, SEG], BF16)
                klnr = sbP("klnr", [1, SEG], F32)
                krstd_r = sbP("krstd_r", [1, SEG], F32)
                krstd_bc = sbP("krstd_bc", [128, SEG], BF16)
                v2_bf = sbP("v2_bf", [128, KVD], BF16)
                ss_c = sbP("ss_c", [128, KVH], F32)
                ssm_c = sbP("ssm_c", [128, KVH], F32)
                sln_c = sbP("sln_c", [128, KVH], F32)
                vrstd_c = sbP("vrstd_c", [128, KVH], F32)
                vn_buf = sbP("vn_buf", [128, KVH * D], BF16)
                pA = psP("pA", [1, SEG])
                pB = psP("pB", [1, SEG])
                pC = psP("pC", [128, SEG])
                pE = psP("pE", [128, SEG])
                pF = psP("pF", [128, SEG])
                pD = psP("pD", [128, SEG])
                pG = psP("pG", [1, SEG])
                pTR = psP("pTR", [128, D], BF16)

                def xnT_src(s):
                    return xnT, (s - own_s0) * SEG, QL

                def xnt_ap(s, cc):
                    xnt, xb, xstr = xnT_src(s)
                    return xnt[:, xb + cc * xstr: xb + cc * xstr + SEG]

                def xnt_stat_ap(s, cc, j):
                    xnt, xb, xstr = xnT_src(s)
                    return xnt[:, xb + cc * xstr + j * D: xb + cc * xstr + (j + 1) * D]

                def vslot(s, j):
                    return pE if (KVH + j) % 2 == 0 else pF

                def kslot(kv):
                    return pE if kv % 2 == 0 else pF

                with Block_() as blk:
                    POOL_CC = CC // 4
                    DVE_CC = CC - POOL_CC

                    @blk.gpsimd
                    def _(g):
                        en = "gpsimd"
                        g.dma_start(
                            xA[0][:, :].rearrange("p (cc n) -> p cc n", n=SEG),
                            xT[:, 0:SEG].rearrange("(cc p) n -> p cc n", p=128),
                        ).then_inc(S["dXA0"], 16)
                        g.dma_start(wk_bf[:, :].rearrange("p (cc o) -> p cc o", o=KVD),
                                    wk.rearrange("(cc p) o -> p cc o", p=128)
                                    ).then_inc(S["dWKV"], 16)
                        g.dma_start(wv_bf[:, :].rearrange("p (cc o) -> p cc o", o=KVD),
                                    wv.rearrange("(cc p) o -> p cc o", p=128)
                                    ).then_inc(S["dWKV"], 16)
                        for s in range(PSEGN):
                            xnt, xb, xstr = xnT_src(s)
                            if POOL_CC:
                                g.wait_ge(S["pSTh"], 2 * s + 2)
                                g.wait_ge(S["aBCE"], 2 * s + 2)
                                for ci in range(POOL_CC):
                                    cc = DVE_CC + ci
                                    ins = g.tensor_tensor(
                                        x2h[:, 0:SEG], xA[0][:, cc * SEG:(cc + 1) * SEG],
                                        rstd_bc[:, :], ALU.mult)
                                    fence(en, g, ins)
                                    ins = g.tensor_tensor(
                                        xnt[:, xb + cc * xstr: xb + cc * xstr + SEG],
                                        x2h[:, 0:SEG], mrstd_bc[:, :], ALU.subtract)
                                    if ci == POOL_CC - 1:
                                        ins.then_inc(S["pAPP"], 1)
                                    else:
                                        fence(en, g, ins)
                            else:
                                g.sem_inc(S["pAPP"], 1)
                            if s + 1 < PSEGN:
                                g.wait_ge(S["dAPP"], s + 1)
                                g.wait_ge(S["pAPP"], s + 1)
                                g.dma_start(
                                    xA[0][:, :].rearrange("p (cc n) -> p cc n", n=SEG),
                                    xT[:, (s + 1) * SEG:(s + 2) * SEG].rearrange(
                                        "(cc p) n -> p cc n", p=128),
                                ).then_inc(S["dXA0"], 16)
                        g.sem_inc(S["bar1"], 1)
                        g.dma_start(
                            wqh[:, :].rearrange("p (cc o) -> p cc o", o=D),
                            wq[:, 0:D].rearrange("(cc p) o -> p cc o", p=128),
                        ).then_inc(S["dWQH"], 16)
                        g.wait_ge(S["dK"], PSEGN * KVH)
                        g.wait_ge(S["aV"], PSEGN * SEGC)
                        g.dma_start(st_k.rearrange("(kv d) t -> d kv t", d=128),
                                    kT_own[:, :].rearrange("d (kv t) -> d kv t", t=QL)
                                    ).then_inc(S["dGK"], 16)
                        g.dma_start(st_v.rearrange("(j p) o -> p j o", p=128),
                                    vT_own[:, :].rearrange("p (j o) -> p j o", o=KVD)
                                    ).then_inc(S["dGV"], 16)
                        g.wait_ge(S["dGK"], 16)
                        g.wait_ge(S["dGV"], 16)
                        if NR > 1:
                            groups = [[gr * NR + i for i in range(NR)]
                                      for gr in range(NCORES // NR)]
                            g.collective_compute(
                                "AllGather", ALU.bypass, replica_groups=groups,
                                ins=[st_k.opt()], outs=[G_k.opt()],
                            ).then_inc(S["ccK"], 1)
                            g.collective_compute(
                                "AllGather", ALU.bypass, replica_groups=groups,
                                ins=[st_v.opt()], outs=[G_v.opt()],
                            ).then_inc(S["ccV"], 1)
                            gk_src, gv_src = G_k, G_v
                        else:
                            gk_src, gv_src = st_k, st_v

                    @blk.sync
                    def _(sy):
                        if NR > 1:
                            sy.wait_ge(S["ccK"], 1)
                            sy.wait_ge(S["ccV"], 1)
                        else:
                            sy.wait_ge(S["dGK"], 16)
                            sy.wait_ge(S["dGV"], 16)
                        gks = G_k if NR > 1 else st_k
                        gvs = G_v if NR > 1 else st_v
                        QT_ = QL // D
                        for r in range(NR):
                            sy.dma_start(
                                kT[:, :].rearrange("d (kv n) -> d kv n", n=NN)[
                                    :, :, r * QL:(r + 1) * QL],
                                gks[r * KVH * D:(r + 1) * KVH * D, :].rearrange(
                                    "(kv d) t -> d kv t", d=128)
                            ).then_inc(S["dRK"], 16)
                            sy.dma_start(
                                vT[:, (r * QT_) * KVD:((r + 1) * QT_) * KVD].rearrange(
                                    "p (j o) -> p j o", o=KVD),
                                gvs[r * QL:(r + 1) * QL, :].rearrange(
                                    "(j p) o -> p j o", p=128)
                            ).then_inc(S["dRV"], 16)


                    @blk.tensor
                    def _(t):
                        t.wait_ge(S["gI"], 8)
                        last = [None]
                        for s in range(PSEGN):
                            t.wait_ge(S["dXA0"], 16 * (s + 1))
                            if s >= 1:
                                t.wait_ge(S["dVAR"], s)
                            for cc in range(CC):
                                t.matmul(pA[:1, :], ones_bf[:, :],
                                         xA[0][:, cc * SEG:(cc + 1) * SEG],
                                         start=(cc == 0), stop=(cc == CC - 1))
                            HCC = CC // 2
                            for half in range(2):
                                t.wait_ge(S["aX2"], 2 * s + half + 1)
                                for c2 in range(HCC):
                                    cc = half * HCC + c2
                                    ins = t.matmul(pB[:1, :], ones_bf[:, :],
                                                   x2h[:, c2 * SEG:(c2 + 1) * SEG],
                                                   start=(cc == 0), stop=(cc == CC - 1))
                                ins.then_inc(S["pSTh"], 1)
                            t.wait_ge(S["aRSTD"], s + 1)
                            if s >= 1:
                                t.wait_ge(S["aBCE"], 2 * s)
                            t.matmul(pC[:, :], onesr_f[:1, :], rstd_r[:1, :],
                                     start=True, stop=True).then_inc(S["pBC"], 1)
                            t.wait_ge(S["dMR"], s + 1)
                            t.wait_ge(S["aBCE"], 2 * s + 1)
                            t.matmul(pC[:, :], onesr_f[:1, :], mrstd_r[:1, :],
                                     start=True, stop=True).then_inc(S["pBC"], 1)
                            if s == 0:
                                t.wait_ge(S["dWKV"], 32)
                            for kv in range(KVH):
                                idx = cnt_pK(s, kv)
                                t.wait_ge(S["dAPP"], s + 1)
                                t.wait_ge(S["pAPP"], s + 1)
                                slot = kslot(kv)
                                if kv >= 2:
                                    t.wait_ge(S["dK"], idx - 2)
                                elif s >= 1:
                                    t.wait_ge(S["aV"], cnt_pV(s - 1, SEGC - 2 + kv))
                                for cc in range(CC):
                                    ins = t.matmul(
                                        slot[:, :],
                                        wk_bf[:, cc * KVD + kv * D: cc * KVD + (kv + 1) * D],
                                        xnt_ap(s, cc),
                                        start=(cc == 0), stop=(cc == CC - 1))
                                ins.then_inc(S["pK"], 1)
                                t.wait_ge(S["aK2"], idx)
                                if idx > 1:
                                    t.wait_ge(S["aKR"], idx - 1)
                                t.matmul(pG[:1, :], ones_bf[:, :], k2_bf[:, :],
                                         start=True, stop=True).then_inc(S["pKMS"], 1)
                                t.wait_ge(S["aKR"], idx)
                                if idx > 1:
                                    t.wait_ge(S["aKBE"], idx - 1)
                                t.matmul(pD[:, :], onesr_f[:1, :], krstd_r[:1, :],
                                         start=True, stop=True).then_inc(S["pKBC"], 1)
                            for j in range(SEGC):
                                jdx = cnt_pV(s, j)
                                slot = vslot(s, j)
                                if j < 2:
                                    t.wait_ge(S["dK"], s * KVH + (KVH - 2 + j) + 1)
                                else:
                                    t.wait_ge(S["aV"], jdx - 2)
                                for cc in range(CC):
                                    ins = t.matmul(
                                        slot[:, :KVD],
                                        xnt_stat_ap(s, cc, j),
                                        wv_bf[:, cc * KVD: (cc + 1) * KVD],
                                        start=(cc == 0), stop=(cc == CC - 1))
                                ins.then_inc(S["pV"], 1)
                            if is_own(s):
                                for j in range(SEGC):
                                    for kv in range(KVH):
                                        ii = own_idx(s, j) * KVH + kv
                                        t.wait_ge(S["aVS"], ii + 1)
                                        if ii > 0:
                                            t.wait_ge(S["dVN"], ii)
                                        last[0] = t.transpose(
                                            pTR[:, :], vn_buf[:, kv * D:(kv + 1) * D],
                                            ident[:, :])
                                        last[0].then_inc(S["pVT"], 1)
                        t.nop().then_inc(S["bar1"], 1)

                    @blk.scalar
                    def _(a):
                        en = "scalar"
                        last = [None]
                        a.wait_ge(S["gI"], 8)
                        for s in range(PSEGN):
                            a.wait_ge(S["dXA0"], 16 * (s + 1))
                            HCC = CC // 2
                            for half in range(2):
                                if 2 * s + half > 0:
                                    a.wait_ge(S["pSTh"], 2 * s + half)
                                a.activation(
                                    x2h[:, :],
                                    xA[0][:, half * HCC * SEG:(half + 1) * HCC * SEG],
                                    AF.Square).then_inc(S["aX2"], 1)
                            a.wait_ge(S["dVAR"], s + 1)
                            ins = a.activation(lnv_r[:1, :], var_r[:1, :], AF.Ln,
                                               bias=eps_ln[:1, :])
                            fence(en, a, ins)
                            if s >= 1:
                                a.wait_ge(S["pBC"], 2 * s)
                                a.wait_ge(S["dMR"], s)
                            a.activation(rstd_r[:1, :], lnv_r[:1, :], AF.Exp,
                                         scale=-0.5).then_inc(S["aRSTD"], 1)
                            a.wait_ge(S["pBC"], 2 * s + 1)
                            if s >= 1:
                                a.wait_ge(S["dAPP"], s)
                            a.activation(rstd_bc[:, :], pC[:, :], AF.Identity
                                         ).then_inc(S["aBCE"], 1)
                            a.wait_ge(S["pBC"], 2 * s + 2)
                            if s >= 1:
                                a.wait_ge(S["dAPP"], s)
                            a.activation(mrstd_bc[:, :], pC[:, :], AF.Identity
                                         ).then_inc(S["aBCE"], 1)
                            for kv in range(KVH):
                                idx = cnt_pK(s, kv)
                                slot = kslot(kv)
                                a.wait_ge(S["pK"], idx)
                                if idx > 1:
                                    a.wait_ge(S["pKMS"], idx - 1)
                                a.activation(k2_bf[:, :], slot[:, :], AF.Square
                                             ).then_inc(S["aK2"], 1)
                                a.wait_ge(S["pKMS"], idx)
                                if idx > 1:
                                    a.wait_ge(S["pKBC"], idx - 1)
                                ins = a.activation(klnr[:1, :], pG[:1, :], AF.Ln,
                                                   scale=1.0 / D, bias=eps_rms[:1, :])
                                fence(en, a, ins)
                                a.activation(krstd_r[:1, :], klnr[:1, :], AF.Exp,
                                             scale=-0.5).then_inc(S["aKR"], 1)
                                a.wait_ge(S["pKBC"], idx)
                                if idx > 1:
                                    a.wait_ge(S["dK"], idx - 1)
                                a.activation(krstd_bc[:, :], pD[:, :], AF.Identity
                                             ).then_inc(S["aKBE"], 1)
                            for j in range(SEGC):
                                jdx = cnt_pV(s, j)
                                slot = vslot(s, j)
                                kpos = s * SEGC + j
                                a.wait_ge(S["pV"], jdx)
                                last[0] = a.activation(
                                    vT_own[:, kpos * KVD:(kpos + 1) * KVD],
                                    slot[:, :KVD], AF.Identity)
                                last[0].then_inc(S["aV"], 1)
                                if is_own(s):
                                    i = own_idx(s, j)
                                    if i > 0:
                                        a.wait_ge(S["dSS"], i)
                                    a.wait_ge(S["aV"], jdx)
                                    a.activation(v2_bf[:, :],
                                                 vT_own[:, kpos * KVD:(kpos + 1) * KVD],
                                                 AF.Square).then_inc(S["aV2"], 1)
                                    a.wait_ge(S["dSS"], i + 1)
                                    if i > 0:
                                        a.wait_ge(S["aVS"], i * KVH)
                                    ins = a.activation(sln_c[:, :], ssm_c[:, :], AF.Ln)
                                    fence(en, a, ins)
                                    a.activation(vrstd_c[:, :], sln_c[:, :], AF.Exp,
                                                 scale=-0.5).then_inc(S["aVR"], 1)
                                    a.wait_ge(S["aVR"], i + 1)
                                    if i > 0:
                                        a.wait_ge(S["pVT"], i * KVH)
                                    for kv in range(KVH):
                                        last[0] = a.activation(
                                            vn_buf[:, kv * D:(kv + 1) * D],
                                            vT_own[:, kpos * KVD + kv * D: kpos * KVD + (kv + 1) * D],
                                            AF.Identity, scale=vrstd_c[:, kv:kv + 1])
                                        last[0].then_inc(S["aVS"], 1)
                        a.drain().then_inc(S["bar1"], 1)

                    @blk.vector
                    def _(v):
                        en = "vector"
                        last = [None]
                        for s in range(PSEGN):
                            v.wait_ge(S["pSTh"], 2 * s + 2)
                            if s >= 1:
                                v.wait_ge(S["pBC"], 2 * s)
                                v.wait_ge(S["aRSTD"], s)
                            ins = v.tensor_scalar(mean_r[:1, :], pA[:1, :], 1.0 / C,
                                                  None, ALU.mult)
                            fence(en, v, ins)
                            ins = v.tensor_scalar(ms_r[:1, :], pB[:1, :], 1.0 / C,
                                                  None, ALU.mult)
                            fence(en, v, ins)
                            ins = v.tensor_tensor(var_r[:1, :], mean_r[:1, :],
                                                  mean_r[:1, :], ALU.mult)
                            fence(en, v, ins)
                            ins = v.tensor_tensor(var_r[:1, :], ms_r[:1, :],
                                                  var_r[:1, :], ALU.subtract)
                            ins.then_inc(S["dVAR"], 1)
                            v.wait_ge(S["aRSTD"], s + 1)
                            v.tensor_tensor(mrstd_r[:1, :], mean_r[:1, :],
                                            rstd_r[:1, :], ALU.mult).then_inc(S["dMR"], 1)
                            xnt, xb, xstr = xnT_src(s)
                            v.wait_ge(S["aBCE"], 2 * s + 2)
                            for cc in range(DVE_CC):
                                ins = v.tensor_tensor(
                                    tmp_bf[:, :], xA[0][:, cc * SEG:(cc + 1) * SEG],
                                    rstd_bc[:, :], ALU.mult)
                                fence(en, v, ins)
                                ins = v.tensor_tensor(
                                    xnt[:, xb + cc * xstr: xb + cc * xstr + SEG],
                                    tmp_bf[:, :], mrstd_bc[:, :], ALU.subtract)
                                if cc < DVE_CC - 1:
                                    fence(en, v, ins)
                            ins.then_inc(S["dAPP"], 1)
                            for kv in range(KVH):
                                idx = cnt_pK(s, kv)
                                slot = kslot(kv)
                                v.wait_ge(S["aKBE"], idx)
                                v.tensor_tensor(
                                    kT_own[:, kv * QL + s * SEG: kv * QL + (s + 1) * SEG],
                                    slot[:, :], krstd_bc[:, :], ALU.mult
                                ).then_inc(S["dK"], 1)
                            if is_own(s):
                                for j in range(SEGC):
                                    i = own_idx(s, j)
                                    v.wait_ge(S["aV2"], i + 1)
                                    if i > 0:
                                        v.wait_ge(S["aVR"], i)
                                    ins = v.tensor_reduce(
                                        ss_c[:, :],
                                        v2_bf[:, :].rearrange("p (kv d) -> p kv d", d=D),
                                        AX.X, ALU.add)
                                    fence(en, v, ins)
                                    v.tensor_scalar(ssm_c[:, :], ss_c[:, :], VN_EPS2,
                                                    None, ALU.max).then_inc(S["dSS"], 1)
                                    for kv in range(KVH):
                                        ii = i * KVH + kv
                                        v.wait_ge(S["pVT"], ii + 1)
                                        tq = (s - own_s0) * SEG + j * D
                                        last[0] = v.tensor_copy(
                                            vnT[:, kv * QL + tq: kv * QL + tq + D],
                                            pTR[:, :])
                                        last[0].then_inc(S["dVN"], 1)
                        v.drain().then_inc(S["bar1"], 1)

            esA = ExitStack()
            with esA:
                sbA = lambda nm, shp, dt: esA.enter_context(nc.sbuf_tensor(nm, shp, dt))
                yT = sbA("yT", [128, H * QL], BF16)
                esQT = ExitStack()
                esQT.__enter__()
                qT = esQT.enter_context(nc.sbuf_tensor("qT", [128, H * QL], BF16))

                esQ = ExitStack()
                with esQ:
                    wq_sl = [esQ.enter_context(nc.sbuf_tensor(f"wq{i}", [128, CC * OSW], BF16))
                             for i in range(2)]
                    q2_bf = esQ.enter_context(nc.sbuf_tensor("q2_bf", [128, QL], BF16))
                    qln_r = esQ.enter_context(nc.sbuf_tensor("qln_r", [1, QL], F32))
                    qrstd_r = esQ.enter_context(nc.sbuf_tensor("qrstd_r", [1, QL], F32))
                    qbc_bf = esQ.enter_context(nc.sbuf_tensor("qbc_bf", [128, QL], BF16))
                    qp = [esQ.enter_context(nc.psum_tensor(f"qp{i}", [128, QL], F32))
                          for i in range(2)]
                    qms = esQ.enter_context(nc.psum_tensor("qms", [1, QL], F32))
                    qbc = esQ.enter_context(nc.psum_tensor("qbc", [128, QL], F32))

                    with Block_() as blk:
                        @blk.gpsimd
                        def _(g):
                            g.wait_ge(S["bar1"], 4)
                            g.wait_ge(S["dGK"], 16)
                            g.wait_ge(S["dGV"], 16)
                            for si in range(NOS):
                                if si >= 2:
                                    g.wait_ge(S["pQ"], (si - 1) * HPS)
                                g.dma_start(
                                    wq_sl[si % 2][:, :].rearrange("p (cc o) -> p cc o", o=OSW),
                                    wq[:, si * OSW:(si + 1) * OSW].rearrange(
                                        "(cc p) o -> p cc o", p=128),
                                ).then_inc(S["dWQ0" if si % 2 == 0 else "dWQ1"], 16)
                            g.wait_ge(S["dRK"], 16 * NR)
                            g.wait_ge(S["dRV"], 16 * NR)
                            g.sem_inc(S["bar1g"], 1)

                        @blk.tensor
                        def _(t):
                            t.wait_ge(S["bar1"], 4)
                            for si in range(NOS):
                                for hh in range(HPS):
                                    h = si * HPS + hh
                                    if h == 0:
                                        t.wait_ge(S["dWQH"], 16)
                                    if h == max(si * HPS, 1):
                                        t.wait_ge(S["dWQ0" if si % 2 == 0 else "dWQ1"],
                                                  16 * (si // 2 + 1))
                                    slot = qp[h % 2]
                                    if h >= 2:
                                        t.wait_ge(S["dQ"], h - 1)
                                    for cc in range(CC):
                                        wsrc = (wqh[:, cc * D:(cc + 1) * D] if h == 0 else
                                                wq_sl[si % 2][:, cc * OSW + hh * D: cc * OSW + (hh + 1) * D])
                                        for (qo, qw) in QH:
                                            ins = t.matmul(
                                                slot[:, qo:qo + qw],
                                                wsrc,
                                                xnT[:, cc * QL + qo: cc * QL + qo + qw],
                                                start=(cc == 0), stop=(cc == CC - 1))
                                    ins.then_inc(S["pQ"], 1)
                                    t.wait_ge(S["aQ2"], h + 1)
                                    if h >= 1:
                                        t.wait_ge(S["aQR"], h)
                                    for (qo, qw) in QH:
                                        ins = t.matmul(qms[:1, qo:qo + qw], ones_bf[:, :],
                                                       q2_bf[:, qo:qo + qw],
                                                       start=True, stop=True)
                                    ins.then_inc(S["pQMS"], 1)
                                    t.wait_ge(S["aQR"], h + 1)
                                    if h >= 1:
                                        t.wait_ge(S["aQBE"], h)
                                    for (qo, qw) in QH:
                                        ins = t.matmul(qbc[:, qo:qo + qw], onesr_f[:1, :],
                                                       qrstd_r[:1, qo:qo + qw],
                                                       start=True, stop=True)
                                    ins.then_inc(S["pQBC"], 1)
                            t.nop().then_inc(S["bar2"], 1)

                        @blk.scalar
                        def _(a):
                            en = "scalar"
                            a.wait_ge(S["bar1"], 4)
                            a.wait_ge(S["dGK"], 16)
                            a.wait_ge(S["dGV"], 16)
                            for h in range(H):
                                slot = qp[h % 2]
                                a.wait_ge(S["pQ"], h + 1)
                                if h >= 1:
                                    a.wait_ge(S["pQMS"], h)
                                a.activation(q2_bf[:, :], slot[:, :], AF.Square
                                             ).then_inc(S["aQ2"], 1)
                                a.wait_ge(S["pQMS"], h + 1)
                                if h >= 1:
                                    a.wait_ge(S["pQBC"], h)
                                ins = a.activation(qln_r[:1, :], qms[:1, :], AF.Ln,
                                                   scale=1.0 / D, bias=eps_rms[:1, :])
                                fence(en, a, ins)
                                a.activation(qrstd_r[:1, :], qln_r[:1, :], AF.Exp,
                                             scale=-0.5).then_inc(S["aQR"], 1)
                                a.wait_ge(S["pQBC"], h + 1)
                                if h >= 1:
                                    a.wait_ge(S["dQ"], h)
                                ins = a.activation(qbc_bf[:, :], qbc[:, :], AF.Identity)
                                ins.then_inc(S["aQBE"], 1)
                            a.drain().then_inc(S["bar2"], 1)

                        @blk.vector
                        def _(v):
                            v.wait_ge(S["bar1"], 4)
                            v.wait_ge(S["dGK"], 16)
                            v.wait_ge(S["dGV"], 16)
                            for h in range(H):
                                slot = qp[h % 2]
                                v.wait_ge(S["aQBE"], h + 1)
                                ins = v.tensor_tensor(qT[:, h * QL:(h + 1) * QL],
                                                      slot[:, :], qbc_bf[:, :], ALU.mult)
                                ins.then_inc(S["dQ"], 1)
                            v.drain().then_inc(S["bar2"], 1)

                esAA = ExitStack()
                with esAA:
                    sbAA = lambda nm, shp, dt: esAA.enter_context(nc.sbuf_tensor(nm, shp, dt))
                    pT = [sbAA(f"pT{i}", [128, QL], BF16) for i in range(8)]
                    acc_d = sbAA("acc_d", [128, QL], F32)
                    if CC * D >= 2 * QL:
                        tmp_d = [wqh[:, 0:QL], wqh[:, QL:2 * QL]]
                    else:
                        tmp_d = [sbAA(f"tmp_d{i}", [128, QL], BF16)[:, :] for i in range(2)]
                    t_bf = sbAA("t_bf", [128, QL], BF16)
                    u_f = sbAA("u_f", [128, QL], F32)
                    r_r = sbAA("r_r", [1, QL], F32)
                    dr_r = sbAA("dr_r", [1, QL], BF16)
                    Sp = [esAA.enter_context(nc.psum_tensor(f"Sp{i}", [128, QL], F32))
                          for i in range(2)]
                    y_ps = esAA.enter_context(nc.psum_tensor("y_ps", [128, QL], F32))
                    TL = esAA.enter_context(nc.psum_tensor("TL", [128, QL], F32))
                    NPAIR = KCH // 2
                    USE_SLOTS = KCH >= 18 and (KCH // 2) >= 11
                    TSLOT = {5: 0, 9: 1, 13: 2, 17: 3} if USE_SLOTS else {}

                    with Block_() as blk:
                        @blk.tensor
                        def _(t):
                            t.wait_ge(S["bar2"], 3)
                            t.wait_ge(S["bar1g"], 1)

                            def tail_mm(t, hp, stage):
                                kv = hp // REP
                                if stage == 0:
                                    t.wait_ge(S["dA"], (hp + 1) * (KCH // 2))
                                    if hp >= 1:
                                        t.wait_ge(S["aR2"], 2 * hp)
                                    for (qo, qw) in QH:
                                        ins = t.matmul(TL[0:1, qo:qo + qw], ones_f[:, :],
                                                       acc_d[:, qo:qo + qw],
                                                       start=True, stop=True)
                                    ins.then_inc(S["pROW"], 1)
                                elif stage == 1:
                                    t.wait_ge(S["dT"], hp + 1)
                                    t.wait_ge(S["aR1"], hp + 1)
                                    for (qo, qw) in QH:
                                        ins = t.matmul(TL[0:1, qo:qo + qw], ones_bf[:, :],
                                                       t_bf[:, qo:qo + qw],
                                                       start=True, stop=True)
                                    ins.then_inc(S["pROW"], 1)
                                elif stage == 2:
                                    t.wait_ge(S["aR1"], hp + 1)
                                    t.wait_ge(S["dROWd"], hp + 1)
                                    for (qo, qw) in QH:
                                        ins = t.matmul(TL[:, qo:qo + qw], onesr_f[:1, :],
                                                       r_r[:1, qo:qo + qw],
                                                       start=True, stop=True)
                                    ins.then_inc(S["pBC2"], 1)
                                else:
                                    t.wait_ge(S["dROWd"], hp + 1)
                                    t.wait_ge(S["aR2"], 2 * hp + 1)
                                    for (qo, qw) in QH:
                                        ins = t.matmul(TL[:, qo:qo + qw], onesr_bf[:1, :],
                                                       dr_r[:1, qo:qo + qw],
                                                       start=True, stop=True)
                                    ins.then_inc(S["pBC2"], 1)

                            def emit_pv(t, h, j):
                                kv = h // REP
                                idx = h * KCH + j
                                t.wait_ge(S["aE"], idx + 1)
                                if j == 0 and h >= 1:
                                    t.wait_ge(S["dYE"], h)
                                for (qo, qw) in QH:
                                    ins = t.matmul(
                                        y_ps[:, qo:qo + qw],
                                        vT[:, j * KVD + kv * D: j * KVD + (kv + 1) * D],
                                        pT[idx % 8][:, qo:qo + qw],
                                        start=(j == 0), stop=(j == KCH - 1))
                                ins.then_inc(S["pPV"], 1)

                            for h in range(H):
                                kv = h // REP
                                if h >= 1 and not TSLOT:
                                    for st in range(4):
                                        tail_mm(t, h - 1, st)
                                for j in range(KCH):
                                    idx = h * KCH + j
                                    if idx >= 2:
                                        t.wait_ge(S["aE"], idx - 1)
                                    for (qo, qw) in QH:
                                        ins = t.matmul(
                                            Sp[j % 2][:, qo:qo + qw],
                                            kT[:, kv * NN + j * D: kv * NN + (j + 1) * D],
                                            qT[:, h * QL + qo: h * QL + qo + qw],
                                            start=True, stop=True)
                                    ins.then_inc(S["pS"], 1)
                                    if h >= 1 and j in TSLOT:
                                        tail_mm(t, h - 1, TSLOT[j])
                                    if j == 2:
                                        emit_pv(t, h, 0)
                                        emit_pv(t, h, 1)
                                    elif j >= 3:
                                        emit_pv(t, h, j - 1)
                                emit_pv(t, h, KCH - 1)
                            for st in range(4):
                                tail_mm(t, H - 1, st)
                            t.nop().then_inc(S["bar3"], 1)

                        @blk.gpsimd
                        def _(g):
                            for h in range(H):
                                for p in range(NPAIR):
                                    pg = h * NPAIR + p
                                    g.wait_ge(S["dP"], pg + 1)
                                    if pg >= 1:
                                        g.wait_ge(S["dA"], pg)
                                    if p == 0:
                                        if h >= 1:
                                            g.wait_ge(S["pROW"], 2 * h - 1)
                                        ins = g.tensor_copy(acc_d[:, :], tmp_d[pg % 2])
                                    else:
                                        ins = g.tensor_tensor(acc_d[:, :], acc_d[:, :],
                                                              tmp_d[pg % 2], ALU.add)
                                    ins.then_inc(S["dA"], 1)
                            g.drain().then_inc(S["bar3"], 1)

                        @blk.scalar
                        def _(a):
                            en = "scalar"
                            a.wait_ge(S["bar2"], 3)
                            a.wait_ge(S["bar1g"], 1)

                            def tail_act(a, hp, stage):
                                a.wait_ge(S["pROW"], 2 * hp + 1)
                                if hp >= 1:
                                    a.wait_ge(S["pBC2"], 2 * hp - 1)
                                    a.wait_ge(S["dROWd"], hp)
                                ins = a.activation(r_r[:1, :], TL[0:1, :], AF.Ln)
                                fence(en, a, ins)
                                a.activation(r_r[:1, :], r_r[:1, :], AF.Exp,
                                             scale=-1.0).then_inc(S["aR1"], 1)

                            ASLOT = {7: 0} if USE_SLOTS else {}
                            for h in range(H):
                                if h >= 1 and not ASLOT:
                                    tail_act(a, h - 1, 0)
                                for j in range(KCH):
                                    idx = h * KCH + j
                                    a.wait_ge(S["pS"], idx + 1)
                                    if idx >= 8:
                                        a.wait_ge(S["pPV"], idx - 7)
                                        a.wait_ge(S["dP"], (idx - 8) // 2 + 1)
                                    a.activation(pT[idx % 8][:, :], Sp[j % 2][:, :],
                                                 AF.Exp, scale=SCALE).then_inc(S["aE"], 1)
                                    if h >= 1 and j in ASLOT:
                                        tail_act(a, h - 1, ASLOT[j])
                                a.wait_ge(S["pPV"], (h + 1) * KCH)
                                if h >= 1:
                                    a.wait_ge(S["dY"], h)
                                a.activation(u_f[:, :], y_ps[:, :], AF.Identity
                                             ).then_inc(S["dYE"], 1)
                            tail_act(a, H - 1, 0)
                            a.drain().then_inc(S["bar3"], 1)

                        @blk.vector
                        def _(v):
                            en = "vector"
                            v.wait_ge(S["bar2"], 3)
                            v.wait_ge(S["bar1g"], 1)

                            def tail_dve(v, hp, stage):
                                kv = hp // REP
                                if stage == 0:
                                    v.wait_ge(S["pROW"], 2 * hp + 2)
                                    v.wait_ge(S["aR1"], hp + 1)
                                    if hp >= 1:
                                        v.wait_ge(S["pBC2"], 2 * hp)
                                    ins = v.tensor_tensor(dr_r[:1, :], TL[0:1, :],
                                                          r_r[:1, :], ALU.mult)
                                    ins.then_inc(S["dROWd"], 1)
                                elif stage == 1:
                                    v.wait_ge(S["pBC2"], 2 * hp + 1)
                                    ins = v.tensor_tensor(u_f[:, :], u_f[:, :],
                                                          TL[:, :], ALU.mult)
                                    ins.then_inc(S["aR2"], 1)
                                elif stage == 2:
                                    v.wait_ge(S["pBC2"], 2 * hp + 2)
                                    v.wait_ge(S["pROW"], 2 * hp + 2)
                                    ins = v.tensor_tensor(t_bf[:, :], TL[:, :],
                                                          vnT[:, kv * QL:(kv + 1) * QL],
                                                          ALU.mult)
                                    ins.then_inc(S["aR2"], 1)
                                else:
                                    v.wait_ge(S["aR2"], 2 * hp + 2)
                                    ins = v.tensor_tensor(yT[:, hp * QL:(hp + 1) * QL],
                                                          u_f[:, :], t_bf[:, :],
                                                          ALU.subtract)
                                    ins.then_inc(S["dY"], 1)

                            VSLOT = {4: 0, 7: 1, 9: 2, 10: 3} if USE_SLOTS else {}
                            NPAIR = KCH // 2
                            for h in range(H):
                                kv = h // REP
                                if h >= 1 and not VSLOT:
                                    for st in range(4):
                                        tail_dve(v, h - 1, st)
                                for p in range(NPAIR):
                                    j0 = 2 * p
                                    idx1 = h * KCH + j0 + 1
                                    pg = h * NPAIR + p
                                    v.wait_ge(S["aE"], idx1 + 1)
                                    if pg >= 2:
                                        v.wait_ge(S["dA"], pg - 1)
                                    g_idx = h * KCH + j0
                                    ins = v.tensor_tensor(tmp_d[pg % 2],
                                                          pT[(g_idx % 8)][:, :],
                                                          pT[(g_idx % 8) + 1][:, :], ALU.add)
                                    ins.then_inc(S["dP"], 1)
                                    if h >= 1 and p in VSLOT:
                                        tail_dve(v, h - 1, VSLOT[p])
                                v.wait_ge(S["dYE"], h + 1)
                                if h >= 1:
                                    v.wait_ge(S["pROW"], 2 * h)
                                    v.wait_ge(S["dY"], h)
                                ins = v.tensor_tensor(t_bf[:, :], u_f[:, :],
                                                      vnT[:, kv * QL:(kv + 1) * QL], ALU.mult)
                                ins.then_inc(S["dT"], 1)
                            for st in range(4):
                                tail_dve(v, H - 1, st)
                            v.drain().then_inc(S["bar3"], 1)

                esQT.__exit__(None, None, None)

                esW = ExitStack()
                with esW:
                    wo_sl = [esW.enter_context(nc.sbuf_tensor(f"wo{i}", [128, H * COLW], BF16))
                             for i in range(2)]
                    xr_b = [esW.enter_context(nc.sbuf_tensor(f"xr{i}", [128, COLW], F32))
                            for i in range(4)]
                    sc_b = [esW.enter_context(nc.sbuf_tensor(f"sc{i}", [128, COLW], F32))
                            for i in range(2)]
                    wps = [esW.enter_context(nc.psum_tensor(f"wp{i}", [128, COLW], F32))
                           for i in range(QT)]

                    with Block_() as blk:
                        @blk.gpsimd
                        def _(g):
                            g.wait_ge(S["pS"], H * KCH)
                            for cq in range(NCQ):
                                if cq >= 2:
                                    g.wait_ge(S["pW"], (cq - 1) * QT)
                                g.dma_start(
                                    wo_sl[cq % 2][:, :].rearrange("p (dc o) -> p dc o", o=COLW),
                                    wo[:, cq * COLW:(cq + 1) * COLW].rearrange(
                                        "(dc p) o -> p dc o", p=128),
                                ).then_inc(S["dWO0" if cq % 2 == 0 else "dWO1"], 16)

                        @blk.sync
                        def _(sy):
                            for cq in range(NCQ):
                                for qt in range(QT):
                                    i = cq * QT + qt
                                    if i < 4:
                                        sy.wait_ge(S["dY"], H)
                                    else:
                                        sy.wait_ge(S["dW"], i - 3)
                                    sy.dma_start(xr_b[i % 4][:, :],
                                                 xr[qt * D:(qt + 1) * D,
                                                    cq * COLW:(cq + 1) * COLW]
                                                 ).then_inc(S[f"dXR{i % 4}"], 16)

                        @blk.scalar
                        def _(a):
                            for cq in range(NCQ):
                                for qt in range(QT):
                                    i = cq * QT + qt
                                    a.wait_ge(S["dW"], i + 1)
                                    a.dma_start(scr1[qt * D:(qt + 1) * D,
                                                     cq * COLW:(cq + 1) * COLW],
                                                sc_b[i % 2][:, :]).then_inc(S["dSC0" if i % 2 == 0 else "dSC1"], 16)

                        @blk.tensor
                        def _(t):
                            t.wait_ge(S["dY"], H)
                            for cq in range(NCQ):
                                t.wait_ge(S["dWO0" if cq % 2 == 0 else "dWO1"], 16 * (cq // 2 + 1))
                                for qt in range(QT):
                                    if cq >= 1:
                                        t.wait_ge(S["dW"], (cq - 1) * QT + qt + 1)
                                    for dc in range(H):
                                        ins = t.matmul(
                                            wps[qt][:, :COLW],
                                            yT[:, dc * QL + qt * D: dc * QL + (qt + 1) * D],
                                            wo_sl[cq % 2][:, dc * COLW:(dc + 1) * COLW],
                                            start=(dc == 0), stop=(dc == H - 1))
                                    ins.then_inc(S["pW"], 1)

                        @blk.vector
                        def _(v):
                            v.wait_ge(S["dY"], H)
                            for cq in range(NCQ):
                                for qt in range(QT):
                                    i = cq * QT + qt
                                    v.wait_ge(S["pW"], i + 1)
                                    v.wait_ge(S[f"dXR{i % 4}"], 16 * (i // 4 + 1))
                                    if i >= 2:
                                        v.wait_ge(S["dSC0" if i % 2 == 0 else "dSC1"], 16 * (i // 2))
                                    ins = v.tensor_tensor(sc_b[i % 2][:, :],
                                                          wps[qt][:, :COLW],
                                                          xr_b[i % 4][:, :], ALU.add)
                                    ins.then_inc(S["dW"], 1)
                            v.drain().then_inc(S["bar4"], 1)

        esF = ExitStack()
        with esF:
            sbF = lambda nm, shp, dt: esF.enter_context(nc.sbuf_tensor(nm, shp, dt))
            gatedT = sbF("gatedT", [128, FFC * QL], BF16)
            wg_sl = [sbF(f"wg{i}", [128, CC * HSW], BF16) for i in range(2)]
            wu_sl = [sbF(f"wu{i}", [128, CC * HSW], BF16) for i in range(2)]
            silu_bf = sbF("silu_bf", [128, QL], BF16)
            wd_sl = [sbF(f"wd{i}", [128, GH * COLW], BF16) for i in range(2)]
            sr_b = [sbF(f"sr{i}", [128, COLW], F32) for i in range(4)]
            ou_b = [sbF(f"ou{i}", [128, COLW], F32) for i in range(2)]

            wd_done = set()

            def wd_dma(g, i):
                cq, hg = divmod(i, NG)
                g.dma_start(
                    wd_sl[i % 2][:, :].rearrange("p (hc o) -> p hc o", o=COLW),
                    wd[hg * GH * D:(hg + 1) * GH * D,
                       cq * COLW:(cq + 1) * COLW].rearrange(
                        "(hc p) o -> p hc o", p=128)
                ).then_inc(S["dWD0" if i % 2 == 0 else "dWD1"], 16)

            esG = ExitStack()
            with esG:
                g_ps = [esG.enter_context(nc.psum_tensor(f"g{i}", [128, QL], F32))
                        for i in range(2)]
                u_ps = [esG.enter_context(nc.psum_tensor(f"u{i}", [128, QL], F32))
                        for i in range(2)]

                with Block_() as blk:
                    @blk.gpsimd
                    def _(g):
                        NQ_ = NCQ * QT
                        for hs in range(NHS):
                            if hs == 0:
                                g.wait_ge(S["pW"], NQ_)
                            elif hs >= 2:
                                g.wait_ge(S["pGU"], (hs - 1) * HPG)
                            g.dma_start(
                                wg_sl[hs % 2][:, :].rearrange("p (cc o) -> p cc o", o=HSW),
                                wg[:, hs * HSW:(hs + 1) * HSW].rearrange(
                                    "(cc p) o -> p cc o", p=128)).then_inc(
                                S["dWG0" if hs % 2 == 0 else "dWG1"], 16)
                            if hs == 1:
                                g.wait_ge(S["dW"], NQ_)
                            g.dma_start(
                                wu_sl[hs % 2][:, :].rearrange("p (cc o) -> p cc o", o=HSW),
                                wu[:, hs * HSW:(hs + 1) * HSW].rearrange(
                                    "(cc p) o -> p cc o", p=128)).then_inc(
                                S["dWG0" if hs % 2 == 0 else "dWG1"], 16)
                            if hs == 1 and NCQ * NG >= 1:
                                g.wait_ge(S["dSC0"], 16 * ((NQ_ + 1) // 2))
                                g.wait_ge(S["dSC1"], 16 * (NQ_ // 2))
                                wd_done.add(0)
                                wd_dma(g, 0)
                            elif hs == 3 and NCQ * NG >= 2:
                                wd_done.add(1)
                                wd_dma(g, 1)

                    @blk.tensor
                    def _(t):
                        t.wait_ge(S["bar4"], 1)
                        for hs in range(NHS):
                            t.wait_ge(S["dWG0" if hs % 2 == 0 else "dWG1"], 32 * (hs // 2 + 1))
                            for hh in range(HPG):
                                hc = hs * HPG + hh
                                p = hc % 2
                                if hc >= 2:
                                    t.wait_ge(S["dGT"], hc - 1)
                                for cc in range(CC):
                                    for (qo, qw) in QH:
                                        t.matmul(
                                            g_ps[p][:, qo:qo + qw],
                                            wg_sl[hs % 2][:, cc * HSW + hh * D: cc * HSW + (hh + 1) * D],
                                            xnT[:, cc * QL + qo: cc * QL + qo + qw],
                                            start=(cc == 0), stop=(cc == CC - 1))
                                for cc in range(CC):
                                    for (qo, qw) in QH:
                                        ins = t.matmul(
                                            u_ps[p][:, qo:qo + qw],
                                            wu_sl[hs % 2][:, cc * HSW + hh * D: cc * HSW + (hh + 1) * D],
                                            xnT[:, cc * QL + qo: cc * QL + qo + qw],
                                            start=(cc == 0), stop=(cc == CC - 1))
                                ins.then_inc(S["pGU"], 1)

                    @blk.scalar
                    def _(a):
                        for hc in range(FFC):
                            p = hc % 2
                            a.wait_ge(S["pGU"], hc + 1)
                            if hc >= 1:
                                a.wait_ge(S["dGT"], hc)
                            ins = a.activation(silu_bf[:, :], g_ps[p][:, :], AF.Silu)
                            ins.then_inc(S["aSIL"], 1)

                    @blk.vector
                    def _(v):
                        for hc in range(FFC):
                            p = hc % 2
                            v.wait_ge(S["aSIL"], hc + 1)
                            ins = v.tensor_tensor(gatedT[:, hc * QL:(hc + 1) * QL],
                                                  u_ps[p][:, :], silu_bf[:, :], ALU.mult)
                            ins.then_inc(S["dGT"], 1)

            esD = ExitStack()
            with esD:
                dps = [esD.enter_context(nc.psum_tensor(f"d{i}", [128, COLW], F32))
                       for i in range(QT)]

                with Block_() as blk:
                    @blk.gpsimd
                    def _(g):
                        NQ_ = NCQ * QT
                        for i in (0, 1):
                            if i not in wd_done and i < NCQ * NG:
                                g.wait_ge(S["dSC0"], 16 * ((NQ_ + 1) // 2))
                                g.wait_ge(S["dSC1"], 16 * (NQ_ // 2))
                                wd_dma(g, i)
                        for i in range(2, NCQ * NG):
                            cq2, hg2 = divmod(i - 2, NG)
                            if hg2 == NG - 1:
                                g.wait_ge(S["pDN"], cq2 * QT + QT)
                            else:
                                g.wait_ge(S["fWD"], sum(
                                    1 for k in range(i - 1) if k % NG != NG - 1))
                            wd_dma(g, i)

                    @blk.sync
                    def _(sy):
                        NQ_ = NCQ * QT
                        sy.wait_ge(S["dSC0"], 16 * ((NQ_ + 1) // 2))
                        sy.wait_ge(S["dSC1"], 16 * (NQ_ // 2))
                        for cq in range(NCQ):
                            for qt in range(QT):
                                i = cq * QT + qt
                                if i >= 4:
                                    sy.wait_ge(S["dF"], i - 3)
                                sy.dma_start(sr_b[i % 4][:, :],
                                             scr1[qt * D:(qt + 1) * D,
                                                  cq * COLW:(cq + 1) * COLW]
                                             ).then_inc(S[f"dSC2{i % 4}"], 16)

                    @blk.scalar
                    def _(a):
                        NQ_ = NCQ * QT
                        for cq in range(NCQ):
                            for qt in range(QT):
                                i = cq * QT + qt
                                a.wait_ge(S["dF"], i + 1)
                                a.dma_start(out[qt * D:(qt + 1) * D,
                                                cq * COLW:(cq + 1) * COLW],
                                            ou_b[i % 2][:, :]).then_inc(
                                    S["dOUT0" if i % 2 == 0 else "dOUT1"], 16)
                        a.wait_ge(S["dOUT0"], 16 * ((NQ_ + 1) // 2))
                        a.wait_ge(S["dOUT1"], 16 * (NQ_ // 2))

                    @blk.tensor
                    def _(t):
                        t.wait_ge(S["dGT"], FFC)
                        t.wait_ge(S["aSIL"], FFC)
                        for cq in range(NCQ):
                            for hg in range(NG):
                                i = cq * NG + hg
                                t.wait_ge(S["dWD0" if i % 2 == 0 else "dWD1"], 16 * (i // 2 + 1))
                                if hg != NG - 1:
                                    for h16 in range(GH):
                                        h = hg * GH + h16
                                        for qt in range(QT):
                                            if h == 0 and cq >= 1:
                                                t.wait_ge(S["dF"], (cq - 1) * QT + qt + 1)
                                            ins = t.matmul(
                                                dps[qt][:, :COLW],
                                                gatedT[:, h * QL + qt * D: h * QL + (qt + 1) * D],
                                                wd_sl[i % 2][:, h16 * COLW:(h16 + 1) * COLW],
                                                start=(h == 0), stop=False)
                                    ins.then_inc(S["fWD"], 1)
                                else:
                                    for qt in range(QT):
                                        for h16 in range(GH):
                                            h = hg * GH + h16
                                            if h == 0 and cq >= 1:
                                                t.wait_ge(S["dF"], (cq - 1) * QT + qt + 1)
                                            ins = t.matmul(
                                                dps[qt][:, :COLW],
                                                gatedT[:, h * QL + qt * D: h * QL + (qt + 1) * D],
                                                wd_sl[i % 2][:, h16 * COLW:(h16 + 1) * COLW],
                                                start=(h == 0), stop=(h == FFC - 1))
                                            if h == FFC - 1:
                                                ins.then_inc(S["pDN"], 1)

                    @blk.vector
                    def _(v):
                        for cq in range(NCQ):
                            for qt in range(QT):
                                i = cq * QT + qt
                                v.wait_ge(S["pDN"], i + 1)
                                v.wait_ge(S[f"dSC2{i % 4}"], 16 * (i // 4 + 1))
                                if i >= 2:
                                    v.wait_ge(S["dOUT0" if i % 2 == 0 else "dOUT1"], 16 * (i // 2))
                                ins = v.tensor_tensor(ou_b[i % 2][:, :],
                                                      dps[qt][:, :COLW],
                                                      sr_b[i % 4][:, :], ALU.add)
                                ins.then_inc(S["dF"], 1)

    return nc




from concourse.bass_utils import run_bass_kernel_spmd

_C, _NN, _QL, _FF, _H, _KVH, _SEG = 2048, 4096, 1024, 8192, 16, 4, 512
_NB, _NCORE = 2, 8
_CPB = _NCORE // _NB


def kernel(L0_attn_x, L0_attn_Wq, L0_attn_Wk, L0_attn_Wv, L0_attn_Wo,
           L0_ffn_w_gate, L0_ffn_w_up, L0_ffn_w_down):
    x = np.asarray(L0_attn_x, dtype=np.float32)
    w = {
        "wq": np.ascontiguousarray(np.asarray(L0_attn_Wq, np.float32).T),
        "wk": np.ascontiguousarray(np.asarray(L0_attn_Wk, np.float32).T),
        "wv": np.ascontiguousarray(np.asarray(L0_attn_Wv, np.float32).T),
        "wo": np.ascontiguousarray(np.asarray(L0_attn_Wo, np.float32).T),
        "wg": np.ascontiguousarray(np.asarray(L0_ffn_w_gate, np.float32).T),
        "wu": np.ascontiguousarray(np.asarray(L0_ffn_w_up, np.float32).T),
        "wd": np.ascontiguousarray(np.asarray(L0_ffn_w_down, np.float32).T),
    }
    in_maps = []
    for c in range(_NCORE):
        b, r = divmod(c, _CPB)
        q0 = r * _QL
        m = {"xT": np.ascontiguousarray(x[b, q0:q0 + _QL].T),
             "xr": np.ascontiguousarray(x[b, q0:q0 + _QL])}
        m.update(w)
        in_maps.append(m)
    nc = build_core(_C, _NN, _QL, 0, _FF, _H, _KVH, _SEG,
                    NR=_CPB, NCORES=_NCORE)
    res = run_bass_kernel_spmd(nc, in_maps, core_ids=list(range(_NCORE)))
    out = np.empty((_NB, _NN, _C), dtype=np.float32)
    for c in range(_NCORE):
        b, r = divmod(c, _CPB)
        out[b, r * _QL:(r + 1) * _QL, :] = res.results[c]["out"]
    return out
